# version 21
# baseline (speedup 1.0000x reference)
import numpy as np
import ml_dtypes
import concourse.bass as bass
import concourse.mybir as mybir
from concourse.bass_utils import run_bass_kernel_spmd
from contextlib import ExitStack

F32 = mybir.dt.float32
BF16 = mybir.dt.bfloat16
I32 = mybir.dt.int32
AF = mybir.ActivationFunctionType
ALU = mybir.AluOpType
AX = mybir.AxisListType

D = 1024
TOK = 4096
CH = 1024
NCH = TOK // CH
S = 512
KT = 8
HT = 16
EXP = 8
EPS = 1e-5

OF_WQT = 0
OF_RTT = 8192
OF_LV = OF_RTT + 8 * 256
OF_WOT = OF_LV + 2 * 1024
FFNW_COLS = OF_WOT + 8192


class Sched:
    ENGS = ("t", "v", "s", "sy", "g")
    OWN = {"t": "T", "v": "V", "s": "S", "sy": None, "g": None}

    def __init__(self):
        self.q = {e: [] for e in self.ENGS}
        self.cnt = {"T": 0, "V": 0, "S": 0, "G": 0,
                    "DX": 0, "DFN": 0, "DW1": 0, "DW2": 0,
                    "DO0": 0, "DO1": 0, "DR0": 0, "DR1": 0}
        self.bank = {i: ("T", 0) for i in range(8)}

    def emit(self, eng, fn, waits=(), inc=None, inc_by=1):
        ws = tuple((sn, vv) for (sn, vv) in waits if vv)
        self.q[eng].append((ws, fn, inc, inc_by))
        if inc is not None:
            self.cnt[inc] += inc_by
        return self.cnt[inc] if inc is not None else None

    def mark(self, sem):
        return self.cnt[sem]

    def play(self, eng, engobj, sems):
        last = {}
        for waits, fn, inc, inc_by in self.q[eng]:
            for sem_name, val in waits:
                if val is None or val <= 0:
                    continue
                if last.get(sem_name, -1) >= val:
                    continue
                engobj.wait_ge(sems[sem_name], val)
                last[sem_name] = val
            inst = fn(engobj)
            if inc is not None:
                inst.then_inc(sems[inc], inc_by)


def build_nc(nch=NCH, stop_after="full", mask_stop="full"):
    nc = bass.Bass("TRN2", target_bir_lowering=False)

    xT = nc.dram_tensor("xT", [D, TOK], F32, kind="ExternalInput").ap()
    w1 = nc.dram_tensor("w1", [EXP, D, 2048], F32, kind="ExternalInput").ap()
    w2 = nc.dram_tensor("w2", [EXP, 2048, D], F32, kind="ExternalInput").ap()
    b1t = nc.dram_tensor("b1t", [EXP, 128, HT], F32, kind="ExternalInput").ap()
    b2c = nc.dram_tensor("b2c", [EXP, 128, KT], F32, kind="ExternalInput").ap()
    wg = nc.dram_tensor("wg", [KT, 128, EXP], F32, kind="ExternalInput").ap()
    bg = nc.dram_tensor("bg", [1, EXP], F32, kind="ExternalInput").ap()
    wqt = nc.dram_tensor("wqt", [D, D], F32, kind="ExternalInput").ap()
    rtt = nc.dram_tensor("rtt", [D, 256], F32, kind="ExternalInput").ap()
    msel = nc.dram_tensor("msel", [256, 512], BF16, kind="ExternalInput").ap()
    lv = nc.dram_tensor("lv", [256, D], F32, kind="ExternalInput").ap()
    wot = nc.dram_tensor("wot", [D, D], F32, kind="ExternalInput").ap()
    wob = nc.dram_tensor("wob", [128, KT], F32, kind="ExternalInput").ap()
    idf = nc.dram_tensor("idf", [128, 128], F32, kind="ExternalInput").ap()
    onesc = nc.dram_tensor("onesc", [128, 1], BF16, kind="ExternalInput").ap()
    epsc = nc.dram_tensor("epsc", [1, 1], F32, kind="ExternalInput").ap()
    onesr = nc.dram_tensor("onesr", [1, 128], BF16, kind="ExternalInput").ap()
    seli = nc.dram_tensor("seli", [EXP, EXP * 128], BF16, kind="ExternalInput").ap()
    out = nc.dram_tensor("out", [D, TOK], F32, kind="ExternalOutput").ap()

    B = Sched()

    with ExitStack() as stack:
        E = stack.enter_context
        x_sb = E(nc.sbuf_tensor("x_sb", [128, KT * CH], F32))
        xbln = E(nc.sbuf_tensor("xbln", [128, KT * CH], BF16))
        wA = E(nc.sbuf_tensor("wA", [128, KT * 2048], BF16))
        wB = E(nc.sbuf_tensor("wB", [128, HT * 1024], BF16))
        ffnw = E(nc.sbuf_tensor("ffnw", [128, FFNW_COLS], BF16))
        arena = E(nc.sbuf_tensor("arena", [128, HT * S], BF16))
        lpT = E(nc.sbuf_tensor("lpT", [128, 2 * S], BF16))
        zqb = E(nc.sbuf_tensor("zqb", [128, 4 * S], BF16))
        scab = E(nc.sbuf_tensor("scab", [128, S], BF16))
        sceb = E(nc.sbuf_tensor("sceb", [128, S], BF16))
        sctb = E(nc.sbuf_tensor("sctb", [128, S], BF16))
        msel_sb = E(nc.sbuf_tensor("msel_sb", [128, 2 * 512], BF16))
        sqb = E(nc.sbuf_tensor("sqb", [128, 4 * S], BF16))
        tmpf = E(nc.sbuf_tensor("tmpf", [128, 2 * S], F32))
        rep = E(nc.sbuf_tensor("rep", [128, CH], F32))
        rowbuf = E(nc.sbuf_tensor("rowbuf", [1, 1024], F32))
        repb = E(nc.sbuf_tensor("repb", [128, 2 * S], BF16))
        normb = E(nc.sbuf_tensor("normb", [128, 2 * S], BF16))
        rowb16 = E(nc.sbuf_tensor("rowb16", [1, 1024], BF16))
        smb = E(nc.sbuf_tensor("smb", [128, 8 * EXP], F32))
        smeb = E(nc.sbuf_tensor("smeb", [128, 8 * EXP], F32))
        rmeb = E(nc.sbuf_tensor("rmeb", [128, 8], F32))
        rsbb = E(nc.sbuf_tensor("rsbb", [128, 8], F32))
        maskb = E(nc.sbuf_tensor("maskb", [128, 8 * EXP], F32))
        mTb = E(nc.sbuf_tensor("mTb", [EXP, CH], BF16))
        b1t_sb = E(nc.sbuf_tensor("b1t_sb", [128, EXP * HT], F32))
        b2c_sb = E(nc.sbuf_tensor("b2c_sb", [128, EXP * KT], F32))
        wg_sb = E(nc.sbuf_tensor("wg_sb", [128, KT * EXP], BF16))
        wob_sb = E(nc.sbuf_tensor("wob_sb", [128, KT], F32))
        bg_sb = E(nc.sbuf_tensor("bg_sb", [1, EXP], F32))
        bgb = E(nc.sbuf_tensor("bgb", [1, EXP], BF16))
        idf_sb = E(nc.sbuf_tensor("idf_sb", [128, 128], F32))
        onesc_sb = E(nc.sbuf_tensor("onesc_sb", [128, 1], BF16))
        epsc_sb = E(nc.sbuf_tensor("epsc_sb", [1, 1], F32))
        onesr_sb = E(nc.sbuf_tensor("onesr_sb", [1, 128], BF16))
        sel_sb = E(nc.sbuf_tensor("sel_sb", [EXP, EXP * 128], BF16))
        P = [E(nc.psum_tensor(f"p{i}", [128, S], F32)) for i in range(8)]
        sem_T = E(nc.semaphore("semT"))
        sem_V = E(nc.semaphore("semV"))
        sem_S = E(nc.semaphore("semS"))
        sem_DX = E(nc.semaphore("semDX"))
        sem_DFN = E(nc.semaphore("semDFN"))
        sem_DW1 = E(nc.semaphore("semDW1"))
        sem_DW2 = E(nc.semaphore("semDW2"))
        sem_G = E(nc.semaphore("semG"))
        sem_DO0 = E(nc.semaphore("semDO0"))
        sem_DO1 = E(nc.semaphore("semDO1"))
        sem_DR0 = E(nc.semaphore("semDR0"))
        sem_DR1 = E(nc.semaphore("semDR1"))
        block = E(nc.Block())

        sems = {"T": sem_T, "V": sem_V, "S": sem_S, "G": sem_G, "DX": sem_DX,
                "DFN": sem_DFN, "DW1": sem_DW1, "DW2": sem_DW2,
                "DO0": sem_DO0, "DO1": sem_DO1, "DR0": sem_DR0, "DR1": sem_DR1}

        def x_ap(k, j0, n):
            return x_sb[:, k * CH + j0:k * CH + j0 + n]

        def xb_ap(k, j0, n):
            return xbln[:, k * CH + j0:k * CH + j0 + n]

        def w1_ap(k, h):
            return wA[:, k * 2048 + h * 128:k * 2048 + h * 128 + 128]

        def w2_ap(q, dt):
            return wB[:, q * 1024 + dt * 128:q * 1024 + dt * 128 + 128]

        def wqt_ap(k, et):
            return ffnw[:, OF_WQT + k * 1024 + et * 128:OF_WQT + k * 1024 + et * 128 + 128]

        def rtt_ap(et, nt):
            c0 = OF_RTT + et * 256 + nt * 128
            return ffnw[:, c0:c0 + 128]

        def msel_ap(nt, kind, lt):
            c0 = nt * 512 + (0 if kind == "z" else 256) + lt * 128
            return msel_sb[:, c0:c0 + 128]

        def zq_z(nt):
            return zqb[:, nt * S:(nt + 1) * S]

        def zq_q(nt):
            return zqb[:, (2 + nt) * S:(3 + nt) * S]

        def sca(nt):
            return scab[:, 0:S]

        def sct(nt):
            return sctb[:, 0:S]

        def lv_ap(l, dt):
            return ffnw[:, OF_LV + l * 1024 + dt * 128:OF_LV + l * 1024 + dt * 128 + 128]

        def wot_ap(k, dt):
            return ffnw[:, OF_WOT + k * 1024 + dt * 128:OF_WOT + k * 1024 + dt * 128 + 128]

        def hp_ap(h, n=S):
            return arena[:, h * S:h * S + n]

        def ret_ap(k, n=S):
            return arena[:, KT * S + k * S:KT * S + k * S + n]

        def dma3(dst2d, c0, c1, kk, src3d):
            dst = dst2d[:, c0:c1].rearrange("p (k n) -> p k n", k=kk)
            src = src3d.transpose([1, 0, 2])
            return dst, src

        def bankw(i):
            sem, m = B.bank[i]
            return (sem, m)

        for dst, src in ((idf_sb[:], idf), (onesc_sb[:], onesc), (onesr_sb[:], onesr),
                         (sel_sb[:], seli), (wob_sb[:], wob), (bg_sb[:], bg),
                         (epsc_sb[:], epsc)):
            B.emit("sy", lambda sy, d=dst, s=src: sy.dma_start(out=d, in_=s),
                   inc="DX", inc_by=16)
        d, s0 = dma3(b1t_sb, 0, EXP * HT, EXP, b1t)
        B.emit("sy", lambda sy, d=d, s=s0: sy.dma_start(out=d, in_=s), inc="DX", inc_by=16)
        d, s0 = dma3(b2c_sb, 0, EXP * KT, EXP, b2c)
        B.emit("sy", lambda sy, d=d, s=s0: sy.dma_start(out=d, in_=s), inc="DX", inc_by=16)
        m_const = B.mark("DX")
        d = x_sb[:].rearrange("p (k j) -> p k j", k=KT)
        s0 = xT.rearrange("(k p) t -> k p t", p=128)[:, :, 0:CH].transpose([1, 0, 2])
        B.emit("sy", lambda sy, d=d, s=s0: sy.dma_start(out=d, in_=s), inc="DX", inc_by=16)
        m_x = [[B.mark("DX"), B.mark("DX")]] + [None] * nch

        d, s0 = dma3(ffnw, OF_WQT, OF_WQT + 8192, KT, wqt.rearrange("(k p) e -> k p e", p=128))
        B.emit("g", lambda g, d=d, s=s0: g.dma_start(out=d, in_=s), inc="DFN", inc_by=16)
        d, s0 = dma3(ffnw, OF_RTT, OF_RTT + 2048, KT, rtt.rearrange("(k p) n -> k p n", p=128))
        B.emit("g", lambda g, d=d, s=s0: g.dma_start(out=d, in_=s), inc="DFN", inc_by=16)
        d, s0 = dma3(ffnw, OF_LV, OF_LV + 2048, 2, lv.rearrange("(l p) e -> l p e", p=128))
        B.emit("g", lambda g, d=d, s=s0: g.dma_start(out=d, in_=s), inc="DFN", inc_by=16)
        d, s0 = dma3(ffnw, OF_WOT, OF_WOT + 8192, KT, wot.rearrange("(k p) e -> k p e", p=128))
        B.emit("g", lambda g, d=d, s=s0: g.dma_start(out=d, in_=s), inc="DFN", inc_by=16)
        d, s0 = dma3(wg_sb, 0, KT * EXP, KT, wg)
        B.emit("g", lambda g, d=d, s=s0: g.dma_start(out=d, in_=s), inc="DFN", inc_by=16)
        d, s0 = dma3(msel_sb, 0, 2 * 512, 2, msel.rearrange("(nt p) e -> nt p e", p=128))
        B.emit("g", lambda g, d=d, s=s0: g.dma_start(out=d, in_=s), inc="DFN", inc_by=16)
        m_ffnw = B.mark("DFN")

        B.emit("s", lambda s: s.mul(ffnw[:, OF_LV:OF_LV + 2048], ffnw[:, OF_LV:OF_LV + 2048],
                                    0.5), waits=[("DFN", m_ffnw)], inc="S")
        m_lvscale = B.mark("S")
        B.emit("v", lambda v: v.tensor_scalar_mul(wob_sb[:], wob_sb[:], 0.5),
               waits=[("DX", m_x[0][0])], inc="V")
        B.emit("v", lambda v: v.tensor_copy(bgb[:], bg_sb[:]), waits=[("DX", m_x[0][0])], inc="V")
        m_pro_v = B.mark("V")

        state = {"wA": None, "wB": None}
        mw1 = {}
        mw2 = {}

        def load_w1(e, wait_T):
            if state["wA"] == e:
                return
            d = wA[:].rearrange("p (k h) -> p k h", k=KT)
            s0 = w1[e].rearrange("(k p) h -> k p h", p=128).transpose([1, 0, 2])
            B.emit("g", lambda g, d=d, s=s0: g.dma_start(out=d, in_=s),
                   waits=[("T", wait_T)], inc="DW1", inc_by=16)
            state["wA"] = e
            mw1[e] = B.mark("DW1")

        def load_w2(e, wait_T):
            if state["wB"] == e:
                return
            d = wB[:].rearrange("p (q dd) -> p q dd", q=HT)
            s0 = w2[e].rearrange("(q p) dd -> q p dd", p=128).transpose([1, 0, 2])
            B.emit("g", lambda g, d=d, s=s0: g.dma_start(out=d, in_=s),
                   waits=[("T", wait_T)], inc="DW2", inc_by=16)
            state["wB"] = e
            mw2[e] = B.mark("DW2")

        load_w1(0, 0)
        load_w2(0, 0)

        marks = {
            "xbln_free_T": 0,
            "arena_free_T": 0,
            "rep_free_V": 0,
        }
        xmark = {(k, s): 0 for k in range(KT) for s in range(2)}
        tmark = {0: 0, 1: 0}
        castmark = {(k, s): ("S", 0) for k in range(KT) for s in range(2)}
        ln_sqfree = {j: 0 for j in range(8)}
        ln_rowfree = {"m": []}
        ln_rb16free = {"m": 0}
        ln_repbfree = {"m": 0}
        ln_normbfree = {"m": 0}
        ffn_state = {"zq_free_T": {i: 0 for i in range(4)},
                     "sca_free_V": {0: 0}, "sct_free_V": {0: 0},
                     "sce_free_S": 0, "lpT_free_T": 0}

        def make_LN(c, dep, cast_in=False):
            mxln = [None, None]
            msub_ks = [None, None]

            def rb(c0, c1):
                return rowbuf[0:1, c0:c1]

            def half(s):
                b0, b1 = (4, 5) if s == 0 else (6, 7)
                if cast_in:
                    for k in range(KT):
                        w = [("T", marks["xbln_free_T"]), dep(k, s),
                             ("V", xmark[(k, s)])]
                        B.emit("s", lambda ss, k=k, s=s: ss.copy(
                            xb_ap(k, s * S, S), x_ap(k, s * S, S)), waits=w, inc="S")
                        castmark[(k, s)] = ("S", B.mark("S"))
                for k in range(KT):
                    w = [castmark[(k, s)]]
                    if k == 0:
                        w.append(bankw(b0))
                    B.emit("t", lambda t, s=s, k=k, b0=b0: t.matmul(
                        P[b0][0:1, :], onesc_sb[:], xb_ap(k, s * S, S),
                        start=(k == 0), stop=(k == KT - 1)),
                        waits=w, inc="T" if k == KT - 1 else None)
                msum = B.mark("T")
                for k in range(KT):
                    j = k % 4
                    gw = [castmark[(k, s)], ("T", ln_sqfree[j])]
                    B.emit("v", lambda v, k=k, s=s, j=j: v.tensor_mul(
                        sqb[:, j * S:(j + 1) * S], xb_ap(k, s * S, S),
                        xb_ap(k, s * S, S)), waits=gw, inc="V")
                    msqk = B.mark("V")
                    w = [("V", msqk)]
                    if k == 0:
                        w.append(bankw(b1))
                    B.emit("t", lambda t, k=k, j=j, b1=b1: t.matmul(
                        P[b1][0:1, :], onesc_sb[:], sqb[:, j * S:(j + 1) * S],
                        start=(k == 0), stop=(k == KT - 1)),
                        waits=w, inc="T")
                    ln_sqfree[j] = B.mark("T")
                msumsq = B.mark("T")
                B.emit("s", lambda ss, b0=b0: ss.activation(
                    rb(0, 512), P[b0][0:1, :], AF.Square, scale=1.0 / D),
                    waits=[("T", msum)] + ln_rowfree["m"], inc="S")
                mm2 = B.mark("S")
                B.emit("v", lambda v, b1=b1: v.scalar_tensor_tensor(
                    rb(512, 1024), P[b1][0:1, :], 1.0 / D, rb(0, 512),
                    ALU.mult, ALU.subtract),
                    waits=[("T", msumsq), ("S", mm2)] + ln_rowfree["m"], inc="V")
                mvar = B.mark("V")
                B.bank[b1] = ("V", mvar)
                B.emit("s", lambda ss, b0=b0: ss.mul(
                    rowb16[0:1, 0:512], P[b0][0:1, :], 1.0 / D),
                    waits=[("T", msum), ("T", ln_rb16free["m"])], inc="S")
                mmean = B.mark("S")
                B.bank[b0] = ("S", mmean)
                B.emit("s", lambda ss: ss.activation(
                    rb(0, 512), rb(512, 1024), AF.Ln, bias=epsc_sb[0:1, 0:1]),
                    waits=[("V", mvar)], inc="S")
                mlnv = B.mark("S")
                B.emit("s", lambda ss: ss.activation(
                    rowb16[0:1, 512:1024], rb(0, 512), AF.Exp, scale=-0.5),
                    waits=[("S", mlnv), ("T", ln_rb16free["m"])], inc="S")
                mrstd = B.mark("S")
                ln_rowfree["m"] = [("S", mrstd), ("V", mvar)]
                B.emit("t", lambda t, b0=b0: t.matmul(
                    P[b0][:, :], onesr_sb[:], rowb16[0:1, 512:1024],
                    start=True, stop=True), waits=[("S", mrstd), bankw(b0)])
                B.emit("t", lambda t, b1=b1: t.matmul(
                    P[b1][:, :], onesr_sb[:], rowb16[0:1, 0:512],
                    start=True, stop=True), waits=[("S", mmean), bankw(b1)], inc="T")
                mrep = B.mark("T")
                ln_rb16free["m"] = mrep
                B.emit("s", lambda ss, b1=b1: ss.copy(repb[:, 512:1024], P[b1][:, :]),
                       waits=[("T", mrep), ("V", ln_repbfree["m"])], inc="S")
                mevm = B.mark("S")
                B.bank[b1] = ("S", mevm)
                B.emit("s", lambda ss, b0=b0: ss.copy(repb[:, 0:512], P[b0][:, :]),
                       waits=[("T", mrep), ("V", ln_repbfree["m"])], inc="S")
                mevr = B.mark("S")
                B.bank[b0] = ("S", mevr)
                seq = []
                for k in range(KT):
                    seq.append(("sub", k))
                    if k >= 1:
                        seq.append(("mul", k - 1))
                seq.append(("mul", KT - 1))
                mmulk = {}
                msubk = {}
                for kind, k in seq:
                    if kind == "sub":
                        w = [("S", mevm), castmark[(k, s)]]
                        if k >= 2:
                            w.append(("V", mmulk[k - 2]))
                        else:
                            w.append(("V", ln_normbfree["m"]))
                        B.emit("v", lambda v, k=k, s=s: v.tensor_tensor(
                            normb[:, (k % 2) * S:(k % 2 + 1) * S], xb_ap(k, s * S, S),
                            repb[:, 512:1024], ALU.subtract), waits=w, inc="V")
                        msubk[k] = B.mark("V")
                    else:
                        B.emit("v", lambda v, k=k, s=s: v.tensor_tensor(
                            xb_ap(k, s * S, S), normb[:, (k % 2) * S:(k % 2 + 1) * S],
                            repb[:, 0:512], ALU.mult),
                            waits=[("V", msubk[k]), ("S", mevr)], inc="V")
                        mmulk[k] = B.mark("V")
                mxln[s] = B.mark("V")
                msub_ks[s] = [mmulk[k] for k in range(KT)]
                ln_repbfree["m"] = mxln[s]
                ln_normbfree["m"] = mxln[s]

            return (lambda: half(0)), (lambda: half(1)), mxln, msub_ks

        def make_FFN(c, cast=False, store=False):
            QB = {0: (0, 1), 1: (2, 3)}
            st = ffn_state
            H = {0: {"mfe": {}}, 1: {"mfe": {}}}
            MS = {}
            done = {"q0": False}

            def q_group(s, mids=None):
                qb = QB[s]
                mq = []
                mqe = []
                for et in range(EXP):
                    if mids and et in mids:
                        mids[et]()
                    for k in range(KT):
                        w = [("V", MS[s][k]), ("DFN", m_ffnw)]
                        if k == 0:
                            if et >= 2:
                                w.append(("S", mqe[et - 2]))
                            else:
                                w.append(bankw(qb[et % 2]))
                        B.emit("t", lambda t, et=et, k=k, s=s, qb=qb: t.matmul(
                            P[qb[et % 2]][:, :], wqt_ap(k, et), xb_ap(k, s * S, S),
                            start=(k == 0), stop=(k == KT - 1)),
                            waits=w, inc="T" if k == KT - 1 else None)
                    mq.append(B.mark("T"))
                    ew = [("T", mq[et]), ("T", marks["arena_free_T"])]
                    if s == 1:
                        ew.append(("T", H[0]["m_z_end"]))
                    B.emit("s", lambda ss, et=et, qb=qb: ss.copy(
                        hp_ap(et), P[qb[et % 2]][:, :]), waits=ew, inc="S")
                    mqe.append(B.mark("S"))
                B.bank[qb[0]] = ("S", mqe[6])
                B.bank[qb[1]] = ("S", mqe[7])
                H[s]["mqe"] = mqe

            def z_group(s):
                mz = []
                for nt in range(2):
                    for et in range(EXP):
                        w = [("S", H[s]["mqe"][et])]
                        if et == 0:
                            w.append(bankw(4 + nt))
                        B.emit("t", lambda t, nt=nt, et=et: t.matmul(
                            P[4 + nt][:, :], rtt_ap(et, nt), hp_ap(et),
                            start=(et == 0), stop=(et == EXP - 1)),
                            waits=w, inc="T" if et == EXP - 1 else None)
                    mz.append(B.mark("T"))
                H[s]["mz"] = mz
                H[s]["m_z_end"] = B.mark("T")

            def schain_group(s):
                mcp = {}
                mq2 = {}
                for nt in range(2):
                    B.emit("s", lambda ss, nt=nt: ss.copy(zq_z(nt), P[4 + nt][:, :]),
                           waits=[("T", H[s]["mz"][nt]),
                                  ("T", st["zq_free_T"][nt])], inc="S")
                    mcp[nt] = B.mark("S")
                    B.emit("s", lambda ss, nt=nt: ss.activation(
                        sca(nt), P[4 + nt][:, :], AF.Abs),
                        waits=[("S", mcp[nt]), ("V", st["sca_free_V"][0])], inc="S")
                    mab = B.mark("S")
                    B.bank[4 + nt] = ("S", mab)
                    B.emit("s", lambda ss, nt=nt: ss.activation(
                        sceb[:, :], sca(nt), AF.Exp, scale=-1.0),
                        waits=[("S", mab), ("S", st["sce_free_S"])], inc="S")
                    mex = B.mark("S")
                    B.emit("s", lambda ss, nt=nt: ss.activation(
                        sct(nt), sceb[:, :], AF.Ln, bias=1.0),
                        waits=[("S", mex), ("V", st["sct_free_V"][0])], inc="S")
                    mln = B.mark("S")
                    st["sce_free_S"] = mln
                    B.emit("v", lambda v, nt=nt: v.scalar_tensor_tensor(
                        zq_q(nt), sca(nt), 0.5, sct(nt), ALU.mult, ALU.add),
                        waits=[("S", mln), ("T", st["zq_free_T"][2 + nt])], inc="V")
                    mq2[nt] = B.mark("V")
                    st["sca_free_V"][0] = mq2[nt]
                    st["sct_free_V"][0] = mq2[nt]
                H[s]["mcp"] = mcp
                H[s]["mq2"] = mq2

            def ll_group(s):
                mll = []
                for lt in range(2):
                    ops = [("z", 0), ("z", 1), ("q", 0), ("q", 1)]
                    for i, (kind, nt) in enumerate(ops):
                        dep = (("S", H[s]["mcp"][nt]) if kind == "z"
                               else ("V", H[s]["mq2"][nt]))
                        w = [dep, ("DFN", m_ffnw)]
                        if i == 0:
                            w.append(bankw(6 + lt))
                        B.emit("t", lambda t, kind=kind, nt=nt, lt=lt, i=i: t.matmul(
                            P[6 + lt][:, :], msel_ap(nt, kind, lt),
                            (zq_z(nt) if kind == "z" else zq_q(nt)),
                            start=(i == 0), stop=(i == 3)),
                            waits=w, inc="T" if i == 3 else None)
                    mll.append(B.mark("T"))
                mend = B.mark("T")
                for i in range(4):
                    st["zq_free_T"][i] = mend
                mexp = []
                for lt in range(2):
                    B.emit("s", lambda ss, lt=lt: ss.activation(
                        lpT[:, lt * S:(lt + 1) * S], P[6 + lt][:, :], AF.Exp),
                        waits=[("T", mll[lt]), ("T", st["lpT_free_T"])], inc="S")
                    mexp.append(B.mark("S"))
                    B.bank[6 + lt] = ("S", mexp[lt])
                H[s]["mexp"] = mexp

            def ret_group(s):
                qb = QB[s]
                slot = ret_ap if s == 0 else hp_ap
                mret = []
                mrete = []
                for dt in range(KT):
                    for l in range(2):
                        w = [("S", H[s]["mexp"][l]), ("S", m_lvscale)]
                        if l == 0:
                            if dt >= 2:
                                w.append(("S", mrete[dt - 2]))
                            else:
                                w.append(bankw(qb[dt % 2]))
                        B.emit("t", lambda t, dt=dt, l=l, qb=qb: t.matmul(
                            P[qb[dt % 2]][:, :], lv_ap(l, dt), lpT[:, l * S:(l + 1) * S],
                            start=(l == 0), stop=(l == 1)),
                            waits=w, inc="T" if l == 1 else None)
                    mret.append(B.mark("T"))
                    ew = [("T", mret[dt])]
                    if s == 0:
                        ew.append(("T", marks["arena_free_T"]))
                    else:
                        ew.append(("T", H[1]["m_z_end"]))
                    B.emit("s", lambda ss, dt=dt, qb=qb, slot=slot: ss.copy(
                        slot(dt), P[qb[dt % 2]][:, :]), waits=ew, inc="S")
                    mrete.append(B.mark("S"))
                B.bank[qb[0]] = ("S", mrete[6])
                B.bank[qb[1]] = ("S", mrete[7])
                H[s]["mrete"] = mrete
                H[s]["slot"] = slot
                st["lpT_free_T"] = B.mark("T")

            def f_group(s, dts):
                qb = QB[s]
                slot = H[s]["slot"]
                mfe = H[s]["mfe"]
                for dt in dts:
                    for k in range(KT):
                        w = [("S", H[s]["mrete"][k])]
                        if k == 0:
                            if dt >= 2:
                                w.append(("V", mfe[dt - 2]))
                            else:
                                w.append(bankw(qb[dt % 2]))
                        B.emit("t", lambda t, dt=dt, k=k, qb=qb, slot=slot: t.matmul(
                            P[qb[dt % 2]][:, :], wot_ap(k, dt), slot(k),
                            start=(k == 0), stop=(k == KT - 1)),
                            waits=w, inc="T" if k == KT - 1 else None)
                    mfdt = B.mark("T")
                    B.emit("v", lambda v, dt=dt, s=s, qb=qb: v.scalar_tensor_tensor(
                        x_ap(dt, s * S, S), P[qb[dt % 2]][:, :], wob_sb[:, dt:dt + 1],
                        x_ap(dt, s * S, S), ALU.add, ALU.add),
                        waits=[("T", mfdt), ("V", m_pro_v), ("V", xmark[(dt, s)])],
                        inc="V")
                    mfe[dt] = B.mark("V")
                    xmark[(dt, s)] = B.mark("V")
                    if cast:
                        B.emit("s", lambda ss, dt=dt, s=s: ss.copy(
                            xb_ap(dt, s * S, S), x_ap(dt, s * S, S)),
                            waits=[("V", xmark[(dt, s)])], inc="S")
                        castmark[(dt, s)] = ("S", B.mark("S"))
                    if store:
                        dst = out[dt * 128:(dt + 1) * 128,
                                  c * CH + s * S:c * CH + s * S + S]
                        src = x_ap(dt, s * S, S)
                        B.emit("sy", lambda sy, dst=dst, src=src: sy.dma_start(
                            out=dst, in_=src), waits=[("V", xmark[(dt, s)])],
                            inc="DO%d" % s, inc_by=16)
                if store and dts[-1] == KT - 1:
                    mo = B.mark("DO%d" % s)
                    if c + 1 < nch:
                        for k in range(KT):
                            s0_ = xT[k * 128:(k + 1) * 128,
                                     (c + 1) * CH + s * S:(c + 1) * CH + s * S + S]
                            B.emit("sy", lambda sy, k=k, s=s, s0_=s0_: sy.dma_start(
                                out=x_ap(k, s * S, S), in_=s0_),
                                waits=[("DO%d" % s, mo)], inc="DR%d" % s, inc_by=16)
                        m_x[c + 1][s] = B.mark("DR%d" % s)
                if dts[-1] == KT - 1:
                    B.bank[qb[0]] = ("V", mfe[6])
                    B.bank[qb[1]] = ("V", mfe[7])

            if store and c + 1 < nch:
                m_x[c + 1] = [None, None]

            def q0(msk0):
                MS[0] = msk0
                q_group(0)
                done["q0"] = True

            def rest(msk, f1_hook=None):
                MS[0], MS[1] = msk[0], msk[1]
                if not done["q0"]:
                    q_group(0)
                z_group(0)
                schain_group(0)
                q_group(1, mids={2: lambda: ll_group(0)})
                z_group(1)
                schain_group(1)
                ret_group(0)
                f_group(0, [0, 1, 2, 3])
                ll_group(1)
                f_group(0, [4, 5, 6, 7])
                ret_group(1)
                f_group(1, [0, 1, 2, 3])
                if f1_hook is not None:
                    f1_hook()
                f_group(1, [4, 5, 6, 7])
                marks["arena_free_T"] = B.mark("T")
                marks["xbln_free_T"] = B.mark("T")
            return q0, rest

        def make_MASK(c, mstop="full"):
            mlg = []

            def gate_groups(cb8s, msk_s):
                for cb8 in cb8s:
                    for k in range(KT):
                        w = [("V", msk_s[k]), ("DFN", m_ffnw)]
                        if k == 0 and cb8 == 0:
                            w.append(bankw(0))
                        B.emit("t", lambda t, cb8=cb8, k=k: t.matmul(
                            P[0][:, cb8 * EXP:(cb8 + 1) * EXP],
                            xbln[:, k * CH + cb8 * 128:k * CH + cb8 * 128 + 128],
                            wg_sb[:, k * EXP:(k + 1) * EXP], start=(k == 0), stop=False),
                            waits=w)
                    B.emit("t", lambda t, cb8=cb8: t.matmul(
                        P[0][:, cb8 * EXP:(cb8 + 1) * EXP], onesr_sb[:], bgb[:],
                        start=False, stop=True), waits=[("V", m_pro_v)], inc="T")
                    mlg.append(B.mark("T"))

            def gate0(msk0):
                gate_groups(range(4), msk0)

            def rest(msk):
                if not mlg:
                    gate_groups(range(4), msk[0])
                gate_groups(range(4, 8), msk[1])
                return softmax_part()

            def softmax_part():
                B.emit("s", lambda ss: ss.copy(smb[:, 0:8 * EXP], P[0][:, 0:8 * EXP]),
                       waits=[("T", mlg[7])], inc="S")
                mlge = B.mark("S")
                B.bank[0] = ("S", mlge)
                if mstop == "lg":
                    return None
                def g3(buf):
                    return buf[:, 0:8 * EXP].rearrange("p (g e) -> p g e", g=8)

                def r3(buf):
                    return buf[:, 0:8].unsqueeze(2)

                def r3b(buf):
                    return buf[:, 0:8].unsqueeze(2).broadcast_to([128, 8, EXP])
                B.emit("v", lambda v: v.reduce_max(r3(rmeb), g3(smb), AX.X),
                       waits=[("S", mlge)], inc="V")
                mrm = B.mark("V")
                B.emit("v", lambda v: v.tensor_tensor(
                    g3(smeb), g3(smb), r3b(rmeb), ALU.subtract),
                    waits=[("V", mrm)], inc="V")
                msub = B.mark("V")
                if mstop == "sub":
                    return None
                B.emit("s", lambda ss: ss.activation(smb[:], smeb[:], AF.Exp),
                       waits=[("V", msub)], inc="S")
                mexp = B.mark("S")
                if mstop == "exp":
                    return None
                B.emit("v", lambda v: v.reduce_sum(r3(rsbb), g3(smb), AX.X),
                       waits=[("S", mexp)], inc="V")
                mrs = B.mark("V")
                B.emit("v", lambda v: v.reciprocal(rsbb[:, 0:8], rsbb[:, 0:8]),
                       waits=[("V", mrs)], inc="V")
                mrcp = B.mark("V")
                if mstop == "recip":
                    return None
                B.emit("v", lambda v: v.tensor_tensor(
                    g3(smeb), g3(smb), r3b(rsbb), ALU.mult),
                    waits=[("V", mrcp)], inc="V")
                mgw = B.mark("V")
                B.emit("v", lambda v: v.reduce_max(r3(rmeb), g3(smeb), AX.X),
                       waits=[("V", mgw)], inc="V")
                mrm2 = B.mark("V")
                B.emit("v", lambda v: v.tensor_tensor(
                    g3(maskb), g3(smeb), r3b(rmeb), ALU.is_ge),
                    waits=[("V", mrm2)], inc="V")
                mge = B.mark("V")
                B.emit("v", lambda v: v.tensor_mul(maskb[:], maskb[:], smeb[:]),
                       waits=[("V", mge)], inc="V")
                mmask = B.mark("V")
                return mmask
            return gate0, rest

        def emit_MASK_finish(mmask):
            for cb8 in range(8):
                bank = 4 if cb8 < 4 else 5
                cc = (cb8 % 4) * 128
                w = [("V", mmask)]
                if cb8 % 4 == 0:
                    w.append(bankw(bank))
                B.emit("t", lambda t, cb8=cb8, bank=bank, cc=cc: t.transpose(
                    P[bank][0:EXP, cc:cc + 128], maskb[:, cb8 * EXP:(cb8 + 1) * EXP], idf_sb[:]),
                    waits=w, inc="T" if cb8 == 7 else None)
            mmt = B.mark("T")
            mallv = B.mark("V")
            B.emit("v", lambda v: v.tensor_copy(tmpf[0:EXP, 0:S], P[4][0:EXP, :]),
                   waits=[("T", mmt), ("V", mallv)], inc="V")
            mtc0 = B.mark("V")
            B.emit("v", lambda v: v.tensor_copy(tmpf[0:EXP, S:2 * S], P[5][0:EXP, :]), inc="V")
            mtc1 = B.mark("V")
            B.emit("v", lambda v: v.tensor_copy(mTb[0:EXP, 0:S], tmpf[0:EXP, 0:S]),
                   waits=[("V", mtc0)], inc="V")
            tmark[0] = B.mark("V")
            B.emit("v", lambda v: v.tensor_copy(mTb[0:EXP, S:CH], tmpf[0:EXP, S:2 * S]),
                   waits=[("V", mtc1)], inc="V")
            tmark[1] = B.mark("V")
            mmaskT = B.mark("V")
            B.bank[4] = ("V", mmaskT)
            B.bank[5] = ("V", mmaskT)
            return mmaskT

        def make_ATTN(c):
            order = list(range(EXP)) if c % 2 == 0 else list(range(EXP - 1, -1, -1))
            ST = {"mmaskT": None, "mrepe": None}
            MS = {}

            def emit_mm1(e, s):
                mm1 = []
                mhe = []
                for h in range(HT):
                    for k in range(KT):
                        w = [("DW1", mw1[e]), ("V", MS[s][k])]
                        if k == 0:
                            if h >= 2:
                                w.append(("S", mhe[h - 2]))
                            else:
                                w.append(bankw(h % 2))
                        B.emit("t", lambda t, h=h, k=k, s=s: t.matmul(
                            P[h % 2][:, :], w1_ap(k, h), xb_ap(k, s * S, S),
                            start=(k == 0), stop=(k == KT - 1)),
                            waits=w, inc="T" if k == KT - 1 else None)
                    mm1.append(B.mark("T"))
                    B.emit("s", lambda ss, h=h, e=e: ss.activation(
                        hp_ap(h), P[h % 2][:, :], AF.Relu,
                        bias=b1t_sb[:, e * HT + h:e * HT + h + 1]),
                        waits=[("T", mm1[h]), ("T", marks["arena_free_T"])], inc="S")
                    mhe.append(B.mark("S"))
                B.bank[0] = ("S", mhe[14])
                B.bank[1] = ("S", mhe[15])
                return mhe, B.mark("T")

            def emit_rep(e):
                B.emit("t", lambda t, e=e: t.matmul(
                    P[4][:, :], sel_sb[:, e * 128:(e + 1) * 128], mTb[0:EXP, 0:S],
                    start=True, stop=True),
                    waits=[("V", ST["mmaskT"]), ("V", marks["rep_free_V"]), bankw(4)])
                B.emit("t", lambda t, e=e: t.matmul(
                    P[5][:, :], sel_sb[:, e * 128:(e + 1) * 128], mTb[0:EXP, S:CH],
                    start=True, stop=True), waits=[bankw(5)], inc="T")
                mrepm = B.mark("T")
                B.emit("v", lambda v: v.tensor_copy(rep[:, 0:S], P[4][:, :]),
                       waits=[("T", mrepm), ("V", marks["rep_free_V"])], inc="V")
                B.emit("v", lambda v: v.tensor_copy(rep[:, S:CH], P[5][:, :]), inc="V")
                mrepe = B.mark("V")
                B.bank[4] = ("V", mrepe)
                B.bank[5] = ("V", mrepe)
                ST["mrepe"] = mrepe
                return mrepe

            def emit_mm2(e, s, mhe, midfns, last):
                banks = [2, 3, 6, 7] if midfns else [2, 3]
                nb = len(banks)
                mm2 = {}
                matt = {}
                pend = [None]
                pending = []

                def flush_one(dt):
                    bk = banks[dt % nb]
                    w = [("T", mm2[dt]), ("V", ST["mrepe"]), ("V", tmark[dt % 2])]
                    B.emit("v", lambda v, dt=dt, e=e, s=s, bk=bk: v.scalar_tensor_tensor(
                        tmpf[:, (dt % 2) * S:(dt % 2 + 1) * S], P[bk][:, :],
                        b2c_sb[:, e * KT + dt:e * KT + dt + 1], rep[:, s * S:(s + 1) * S],
                        ALU.add, ALU.mult),
                        waits=w, inc="V")
                    matt[dt] = B.mark("V")
                    if pend[0] is not None:
                        pdt, pfn = pend[0]
                        B.emit("v", pfn, waits=[("V", matt[pdt]),
                                                ("V", xmark[(pdt, s)])], inc="V")
                        xmark[(pdt, s)] = B.mark("V")
                        tmark[pdt % 2] = B.mark("V")
                        if last:
                            B.emit("s", lambda ss, pdt=pdt, s=s: ss.copy(
                                xb_ap(pdt, s * S, S), x_ap(pdt, s * S, S)),
                                waits=[("V", xmark[(pdt, s)])], inc="S")
                            castmark[(pdt, s)] = ("S", B.mark("S"))
                    pend[0] = (dt, (lambda v, dt=dt, s=s: v.tensor_add(
                        x_ap(dt, s * S, S), x_ap(dt, s * S, S),
                        tmpf[:, (dt % 2) * S:(dt % 2 + 1) * S])))

                for dt in range(KT):
                    if midfns and dt in midfns:
                        midfns[dt]()
                    if ST["mrepe"] is not None:
                        while pending:
                            flush_one(pending.pop(0))
                    bk = banks[dt % nb]
                    for q in range(HT):
                        w = [("DW2", mw2[e]), ("S", mhe[q])]
                        if q == 0:
                            if dt >= nb:
                                w.append(("V", matt[dt - nb]))
                            else:
                                w.append(bankw(bk))
                        B.emit("t", lambda t, dt=dt, q=q, bk=bk: t.matmul(
                            P[bk][:, :], w2_ap(q, dt), hp_ap(q),
                            start=(q == 0), stop=(q == HT - 1)),
                            waits=w, inc="T" if q == HT - 1 else None)
                    mm2[dt] = B.mark("T")
                    pending.append(dt)
                    if ST["mrepe"] is not None:
                        while pending:
                            flush_one(pending.pop(0))
                while pending:
                    flush_one(pending.pop(0))
                pdt, pfn = pend[0]
                B.emit("v", pfn, waits=[("V", matt[pdt]), ("V", xmark[(pdt, s)])], inc="V")
                xmark[(pdt, s)] = B.mark("V")
                tmark[pdt % 2] = B.mark("V")
                if last:
                    B.emit("s", lambda ss, pdt=pdt, s=s: ss.copy(
                        xb_ap(pdt, s * S, S), x_ap(pdt, s * S, S)),
                        waits=[("V", xmark[(pdt, s)])], inc="S")
                    castmark[(pdt, s)] = ("S", B.mark("S"))
                for i, b in enumerate(banks):
                    B.bank[b] = ("V", matt[KT - nb + i])

            def mm1_first(msk0):
                MS[0] = msk0
                ST["first"] = emit_mm1(order[0], 0)

            def rest(msk, mask_fn, ln3_half0=None):
                MS[0], MS[1] = msk[0], msk[1]
                for ei, e in enumerate(order):
                    last = (ei == EXP - 1)
                    for s in range(2):
                        if ei == 0 and s == 0:
                            mhe, m_mm1_end = ST["first"]
                            mk = {"mmask": None}

                            def mid_gates():
                                mk["mmask"] = mask_fn()

                            def mid_finish():
                                ST["mmaskT"] = emit_MASK_finish(mk["mmask"])
                                emit_rep(e)
                            ST["mrepe"] = None
                            emit_mm2(e, s, mhe, {2: mid_gates, 4: mid_finish}, last)
                        else:
                            mhe, m_mm1_end = emit_mm1(e, s)
                            if s == 0:
                                emit_rep(e)
                            if s == 1 and last and ln3_half0 is not None:
                                ln3_half0()
                            if s == 1 and ei < EXP - 1:
                                load_w1(order[ei + 1], m_mm1_end)
                            emit_mm2(e, s, mhe, None, last)
                        marks["arena_free_T"] = B.mark("T")
                        if s == 1 and ei < EXP - 1:
                            load_w2(order[ei + 1], marks["arena_free_T"])
                    marks["rep_free_V"] = B.mark("V")
                marks["xbln_free_T"] = B.mark("T")
            return mm1_first, rest

        def done(phase):
            order = ["ln1", "ffn1", "ln2", "mask", "attn", "ln3", "ffn2", "full"]
            return order.index(stop_after) < order.index(phase)

        def lnd(cc):
            def dep(k, s, cc=cc):
                if cc == 0:
                    return ("DX", m_x[cc][s])
                return ("DR%d" % s, m_x[cc][s])
            return dep

        ln1_h0, ln1_h1, ln1_mxln, ln1_msk = make_LN(0, lnd(0), cast_in=True)
        ln1_h0()
        for c in range(nch):
            ffn1_q0, ffn1_rest = make_FFN(c, cast=True)
            ffn1_q0(ln1_msk[0])
            ln1_h1()
            if done("ffn1"):
                break
            ln2_h0, ln2_h1, ln2_mxln, ln2_msk = make_LN(c, None)
            ffn1_rest(ln1_msk, f1_hook=ln2_h0)
            if done("ln2"):
                break
            attn_mm1, attn_rest = make_ATTN(c)
            attn_mm1(ln2_msk[0])
            ln2_h1()
            if done("mask"):
                break
            _, mask_rest = make_MASK(c, mstop=mask_stop)
            ln3_h0, ln3_h1, ln3_mxln, ln3_msk = make_LN(c, None)
            attn_rest(ln2_msk, mask_fn=lambda: mask_rest(ln2_msk),
                      ln3_half0=ln3_h0)
            if done("attn") or done("ln3"):
                break
            ffn2_q0, ffn2_rest = make_FFN(c, cast=False, store=True)
            ffn2_q0(ln3_msk[0])
            ln3_h1()
            if done("ffn2"):
                break
            if c + 1 < nch:
                ln1_h0, ln1_h1, ln1_mxln, ln1_msk = make_LN(
                    c + 1, lnd(c + 1), cast_in=True)
                ffn2_rest(ln3_msk, f1_hook=ln1_h0)
            else:
                ffn2_rest(ln3_msk)

        if stop_after != "full":
            wd = [("V", B.mark("V")), ("S", B.mark("S")), ("T", B.mark("T"))]
            dst = out.rearrange("(k p) t -> k p t", p=128)[:, :, 0:CH].transpose([1, 0, 2])
            src = x_sb[:].rearrange("p (k j) -> p k j", k=KT)
            B.emit("sy", lambda sy, dst=dst, src=src: sy.dma_start(out=dst, in_=src),
                   waits=wd, inc="DX", inc_by=16)
            dst = out.rearrange("(k p) t -> k p t", p=128)[:, :, CH:2 * CH].transpose([1, 0, 2])
            src = xbln[:].rearrange("p (k j) -> p k j", k=KT)
            B.emit("g", lambda g, dst=dst, src=src: g.dma_start(out=dst, in_=src),
                   waits=wd, inc="DX", inc_by=16)

        m_final = B.mark("DX")

        m_do0 = B.mark("DO0")
        m_do1 = B.mark("DO1")

        @block.sync
        def _(sy):
            B.play("sy", sy, sems)
            sy.wait_ge(sem_DX, m_final)
            if m_do0:
                sy.wait_ge(sem_DO0, m_do0)
            if m_do1:
                sy.wait_ge(sem_DO1, m_do1)

        @block.gpsimd
        def _(g):
            B.play("g", g, sems)

        @block.tensor
        def _(t):
            B.play("t", t, sems)

        @block.vector
        def _(v):
            B.play("v", v, sems)

        @block.scalar
        def _(s):
            B.play("s", s, sems)

    return nc


def _host_inputs(L0_attn_Wg, L0_attn_bg, L0_attn_W1, L0_attn_b1, L0_attn_W2, L0_attn_b2,
                 L0_ffn_Wq, L0_ffn_routers, L0_ffn_leaf_values, L0_ffn_Wout_w, L0_ffn_Wout_b):
    f = np.float32
    shared = {
        "w1": np.ascontiguousarray(L0_attn_W1, dtype=f),
        "w2": np.ascontiguousarray(L0_attn_W2, dtype=f),
        "b1t": np.ascontiguousarray(np.asarray(L0_attn_b1, dtype=f).reshape(EXP, HT, 128)
                                    .transpose(0, 2, 1)),
        "b2c": np.ascontiguousarray(np.asarray(L0_attn_b2, dtype=f).reshape(EXP, KT, 128)
                                    .transpose(0, 2, 1)),
        "wg": np.ascontiguousarray(np.asarray(L0_attn_Wg, dtype=f).reshape(KT, 128, EXP)),
        "bg": np.ascontiguousarray(np.asarray(L0_attn_bg, dtype=f).reshape(1, EXP)),
        "wqt": np.ascontiguousarray(np.asarray(L0_ffn_Wq, dtype=f).T),
        "rtt": np.ascontiguousarray(np.concatenate(
            [np.asarray(L0_ffn_routers, dtype=f).T, np.zeros((D, 1), dtype=f)], axis=1)),
        "lv": np.ascontiguousarray(L0_ffn_leaf_values, dtype=f),
        "wot": np.ascontiguousarray(np.asarray(L0_ffn_Wout_w, dtype=f).T),
        "wob": np.ascontiguousarray(np.asarray(L0_ffn_Wout_b, dtype=f).reshape(KT, 128).T),
        "idf": np.eye(128, dtype=f),
        "onesc": np.ones((128, 1), dtype=ml_dtypes.bfloat16),
        "epsc": np.full((1, 1), EPS, dtype=f),
        "onesr": np.ones((1, 128), dtype=ml_dtypes.bfloat16),
    }
    sel = np.zeros((EXP, EXP * 128), dtype=ml_dtypes.bfloat16)
    for e in range(EXP):
        sel[e, e * 128:(e + 1) * 128] = 1
    shared["seli"] = sel
    ms = np.zeros((256, 512), dtype=np.float32)
    for m in range(256):
        for lev in range(8):
            j = (1 << lev) - 1 + (m >> (8 - lev))
            right = (m >> (7 - lev)) & 1
            ms[j, m] = 0.5 if right else -0.5
            ms[j, 256 + m] = -1.0
    shared["msel"] = ms.astype(ml_dtypes.bfloat16)
    return shared


def kernel(L0_attn_x, L0_attn_Wg, L0_attn_bg, L0_attn_W1, L0_attn_b1, L0_attn_W2, L0_attn_b2,
           L0_ffn_Wq, L0_ffn_routers, L0_ffn_leaf_keys, L0_ffn_leaf_values,
           L0_ffn_Wout_w, L0_ffn_Wout_b):
    x = np.asarray(L0_attn_x, dtype=np.float32)
    shared = _host_inputs(L0_attn_Wg, L0_attn_bg, L0_attn_W1, L0_attn_b1, L0_attn_W2,
                          L0_attn_b2, L0_ffn_Wq, L0_ffn_routers, L0_ffn_leaf_values,
                          L0_ffn_Wout_w, L0_ffn_Wout_b)
    nc = build_nc()
    in_maps = []
    for b in range(8):
        m = dict(shared)
        m["xT"] = np.ascontiguousarray(x[b].T)
        in_maps.append(m)
    res = run_bass_kernel_spmd(nc, in_maps, core_ids=list(range(8)))
    outs = [np.asarray(r["out"]).T for r in res.results]
    return np.ascontiguousarray(np.stack(outs, axis=0), dtype=np.float32)


# version 22
# speedup vs baseline: 1.1014x; 1.1014x over previous
import numpy as np
import ml_dtypes
import concourse.bass as bass
import concourse.mybir as mybir
from concourse.bass_utils import run_bass_kernel_spmd
from contextlib import ExitStack

F32 = mybir.dt.float32
BF16 = mybir.dt.bfloat16
I32 = mybir.dt.int32
AF = mybir.ActivationFunctionType
ALU = mybir.AluOpType
AX = mybir.AxisListType

D = 1024
TOK = 4096
CH = 1024
NCH = TOK // CH
S = 512
KT = 8
HT = 16
EXP = 8
EPS = 1e-5

OF_WQT = 0
OF_RTT = 8192
OF_LV = OF_RTT + 8 * 256
OF_WOT = OF_LV + 2 * 1024
FFNW_COLS = OF_WOT + 8192


class Sched:
    ENGS = ("t", "v", "s", "sy", "g")
    OWN = {"t": "T", "v": "V", "s": "S", "sy": None, "g": None}

    def __init__(self):
        self.q = {e: [] for e in self.ENGS}
        self.cnt = {"T": 0, "V": 0, "S": 0, "G": 0,
                    "DX": 0, "DFN": 0, "DW1": 0, "DW2": 0,
                    "DO0": 0, "DO1": 0, "DR0": 0, "DR1": 0}
        self.bank = {i: ("T", 0) for i in range(8)}

    def emit(self, eng, fn, waits=(), inc=None, inc_by=1):
        ws = tuple((sn, vv) for (sn, vv) in waits if vv)
        self.q[eng].append((ws, fn, inc, inc_by))
        if inc is not None:
            self.cnt[inc] += inc_by
        return self.cnt[inc] if inc is not None else None

    def mark(self, sem):
        return self.cnt[sem]

    def play(self, eng, engobj, sems):
        last = {}
        for waits, fn, inc, inc_by in self.q[eng]:
            for sem_name, val in waits:
                if val is None or val <= 0:
                    continue
                if last.get(sem_name, -1) >= val:
                    continue
                engobj.wait_ge(sems[sem_name], val)
                last[sem_name] = val
            inst = fn(engobj)
            if inc is not None:
                inst.then_inc(sems[inc], inc_by)


def build_nc(nch=NCH, stop_after="full", mask_stop="full"):
    nc = bass.Bass("TRN2", target_bir_lowering=False)

    xT = nc.dram_tensor("xT", [D, TOK], F32, kind="ExternalInput").ap()
    w1 = nc.dram_tensor("w1", [EXP, D, 2048], F32, kind="ExternalInput").ap()
    w2 = nc.dram_tensor("w2", [EXP, 2048, D], F32, kind="ExternalInput").ap()
    b1t = nc.dram_tensor("b1t", [EXP, 128, HT], F32, kind="ExternalInput").ap()
    b2c = nc.dram_tensor("b2c", [EXP, 128, KT], F32, kind="ExternalInput").ap()
    wg = nc.dram_tensor("wg", [KT, 128, EXP], F32, kind="ExternalInput").ap()
    bg = nc.dram_tensor("bg", [1, EXP], F32, kind="ExternalInput").ap()
    wqt = nc.dram_tensor("wqt", [D, D], F32, kind="ExternalInput").ap()
    rtt = nc.dram_tensor("rtt", [D, 256], F32, kind="ExternalInput").ap()
    msel = nc.dram_tensor("msel", [256, 512], BF16, kind="ExternalInput").ap()
    lv = nc.dram_tensor("lv", [256, D], F32, kind="ExternalInput").ap()
    wot = nc.dram_tensor("wot", [D, D], F32, kind="ExternalInput").ap()
    wob = nc.dram_tensor("wob", [128, KT], F32, kind="ExternalInput").ap()
    idf = nc.dram_tensor("idf", [128, 128], F32, kind="ExternalInput").ap()
    onesc = nc.dram_tensor("onesc", [128, 1], BF16, kind="ExternalInput").ap()
    epsc = nc.dram_tensor("epsc", [1, 1], F32, kind="ExternalInput").ap()
    onesr = nc.dram_tensor("onesr", [1, 128], BF16, kind="ExternalInput").ap()
    seli = nc.dram_tensor("seli", [EXP, EXP * 128], BF16, kind="ExternalInput").ap()
    out = nc.dram_tensor("out", [D, TOK], F32, kind="ExternalOutput").ap()

    B = Sched()

    with ExitStack() as stack:
        E = stack.enter_context
        x_sb = E(nc.sbuf_tensor("x_sb", [128, KT * CH], F32))
        xbln = E(nc.sbuf_tensor("xbln", [128, KT * CH], BF16))
        wA = E(nc.sbuf_tensor("wA", [128, KT * 2048], BF16))
        wB = E(nc.sbuf_tensor("wB", [128, HT * 1024], BF16))
        ffnw = E(nc.sbuf_tensor("ffnw", [128, FFNW_COLS], BF16))
        arena = E(nc.sbuf_tensor("arena", [128, HT * S], BF16))
        lpT = E(nc.sbuf_tensor("lpT", [128, 2 * S], BF16))
        zqb = E(nc.sbuf_tensor("zqb", [128, 4 * S], BF16))
        scab = E(nc.sbuf_tensor("scab", [128, S], BF16))
        sceb = E(nc.sbuf_tensor("sceb", [128, S], BF16))
        sctb = E(nc.sbuf_tensor("sctb", [128, S], BF16))
        msel_sb = E(nc.sbuf_tensor("msel_sb", [128, 2 * 512], BF16))
        sqb = E(nc.sbuf_tensor("sqb", [128, 4 * S], BF16))
        tmpf = E(nc.sbuf_tensor("tmpf", [128, 2 * S], F32))
        rep = E(nc.sbuf_tensor("rep", [128, CH], F32))
        rowbuf = E(nc.sbuf_tensor("rowbuf", [1, 1024], F32))
        repb = E(nc.sbuf_tensor("repb", [128, 2 * S], BF16))
        normb = E(nc.sbuf_tensor("normb", [128, 2 * S], BF16))
        rowb16 = E(nc.sbuf_tensor("rowb16", [1, 1024], BF16))
        smb = E(nc.sbuf_tensor("smb", [128, 8 * EXP], F32))
        smeb = E(nc.sbuf_tensor("smeb", [128, 8 * EXP], F32))
        rmeb = E(nc.sbuf_tensor("rmeb", [128, 8], F32))
        rsbb = E(nc.sbuf_tensor("rsbb", [128, 8], F32))
        maskb = E(nc.sbuf_tensor("maskb", [128, 8 * EXP], F32))
        mTb = E(nc.sbuf_tensor("mTb", [EXP, CH], BF16))
        b1t_sb = E(nc.sbuf_tensor("b1t_sb", [128, EXP * HT], F32))
        b2c_sb = E(nc.sbuf_tensor("b2c_sb", [128, EXP * KT], F32))
        wg_sb = E(nc.sbuf_tensor("wg_sb", [128, KT * EXP], BF16))
        wob_sb = E(nc.sbuf_tensor("wob_sb", [128, KT], F32))
        bg_sb = E(nc.sbuf_tensor("bg_sb", [1, EXP], F32))
        bgb = E(nc.sbuf_tensor("bgb", [1, EXP], BF16))
        idf_sb = E(nc.sbuf_tensor("idf_sb", [128, 128], F32))
        onesc_sb = E(nc.sbuf_tensor("onesc_sb", [128, 1], BF16))
        epsc_sb = E(nc.sbuf_tensor("epsc_sb", [1, 1], F32))
        onesr_sb = E(nc.sbuf_tensor("onesr_sb", [1, 128], BF16))
        sel_sb = E(nc.sbuf_tensor("sel_sb", [EXP, EXP * 128], BF16))
        P = [E(nc.psum_tensor(f"p{i}", [128, S], F32)) for i in range(8)]
        sem_T = E(nc.semaphore("semT"))
        sem_V = E(nc.semaphore("semV"))
        sem_S = E(nc.semaphore("semS"))
        sem_DX = E(nc.semaphore("semDX"))
        sem_DFN = E(nc.semaphore("semDFN"))
        sem_DW1 = E(nc.semaphore("semDW1"))
        sem_DW2 = E(nc.semaphore("semDW2"))
        sem_G = E(nc.semaphore("semG"))
        sem_DO0 = E(nc.semaphore("semDO0"))
        sem_DO1 = E(nc.semaphore("semDO1"))
        sem_DR0 = E(nc.semaphore("semDR0"))
        sem_DR1 = E(nc.semaphore("semDR1"))
        block = E(nc.Block())

        sems = {"T": sem_T, "V": sem_V, "S": sem_S, "G": sem_G, "DX": sem_DX,
                "DFN": sem_DFN, "DW1": sem_DW1, "DW2": sem_DW2,
                "DO0": sem_DO0, "DO1": sem_DO1, "DR0": sem_DR0, "DR1": sem_DR1}

        def x_ap(k, j0, n):
            return x_sb[:, k * CH + j0:k * CH + j0 + n]

        def xb_ap(k, j0, n):
            return xbln[:, k * CH + j0:k * CH + j0 + n]

        def w1_ap(k, h):
            return wA[:, k * 2048 + h * 128:k * 2048 + h * 128 + 128]

        def w2_ap(q, dt):
            return wB[:, q * 1024 + dt * 128:q * 1024 + dt * 128 + 128]

        def wqt_ap(k, et):
            return ffnw[:, OF_WQT + k * 1024 + et * 128:OF_WQT + k * 1024 + et * 128 + 128]

        def rtt_ap(et, nt):
            c0 = OF_RTT + et * 256 + nt * 128
            return ffnw[:, c0:c0 + 128]

        def msel_ap(nt, kind, lt):
            c0 = nt * 512 + (0 if kind == "z" else 256) + lt * 128
            return msel_sb[:, c0:c0 + 128]

        def zq_z(nt):
            return zqb[:, nt * S:(nt + 1) * S]

        def zq_q(nt):
            return zqb[:, (2 + nt) * S:(3 + nt) * S]

        def sca(nt):
            return scab[:, 0:S]

        def sct(nt):
            return sctb[:, 0:S]

        def lv_ap(l, dt):
            return ffnw[:, OF_LV + l * 1024 + dt * 128:OF_LV + l * 1024 + dt * 128 + 128]

        def wot_ap(k, dt):
            return ffnw[:, OF_WOT + k * 1024 + dt * 128:OF_WOT + k * 1024 + dt * 128 + 128]

        def hp_ap(h, n=S):
            return arena[:, h * S:h * S + n]

        def ret_ap(k, n=S):
            return arena[:, KT * S + k * S:KT * S + k * S + n]

        def dma3(dst2d, c0, c1, kk, src3d):
            dst = dst2d[:, c0:c1].rearrange("p (k n) -> p k n", k=kk)
            src = src3d.transpose([1, 0, 2])
            return dst, src

        def bankw(i):
            sem, m = B.bank[i]
            return (sem, m)

        for dst, src in ((idf_sb[:], idf), (onesc_sb[:], onesc), (onesr_sb[:], onesr),
                         (sel_sb[:], seli), (wob_sb[:], wob), (bg_sb[:], bg),
                         (epsc_sb[:], epsc)):
            B.emit("sy", lambda sy, d=dst, s=src: sy.dma_start(out=d, in_=s),
                   inc="DX", inc_by=16)
        d, s0 = dma3(b1t_sb, 0, EXP * HT, EXP, b1t)
        B.emit("sy", lambda sy, d=d, s=s0: sy.dma_start(out=d, in_=s), inc="DX", inc_by=16)
        d, s0 = dma3(b2c_sb, 0, EXP * KT, EXP, b2c)
        B.emit("sy", lambda sy, d=d, s=s0: sy.dma_start(out=d, in_=s), inc="DX", inc_by=16)
        m_const = B.mark("DX")
        d = x_sb[:].rearrange("p (k j) -> p k j", k=KT)
        s0 = xT.rearrange("(k p) t -> k p t", p=128)[:, :, 0:CH].transpose([1, 0, 2])
        B.emit("sy", lambda sy, d=d, s=s0: sy.dma_start(out=d, in_=s), inc="DX", inc_by=16)
        m_x = [[B.mark("DX"), B.mark("DX")]] + [None] * nch

        d, s0 = dma3(ffnw, OF_WQT, OF_WQT + 8192, KT, wqt.rearrange("(k p) e -> k p e", p=128))
        B.emit("g", lambda g, d=d, s=s0: g.dma_start(out=d, in_=s), inc="DFN", inc_by=16)
        d, s0 = dma3(ffnw, OF_RTT, OF_RTT + 2048, KT, rtt.rearrange("(k p) n -> k p n", p=128))
        B.emit("g", lambda g, d=d, s=s0: g.dma_start(out=d, in_=s), inc="DFN", inc_by=16)
        d, s0 = dma3(ffnw, OF_LV, OF_LV + 2048, 2, lv.rearrange("(l p) e -> l p e", p=128))
        B.emit("g", lambda g, d=d, s=s0: g.dma_start(out=d, in_=s), inc="DFN", inc_by=16)
        d, s0 = dma3(ffnw, OF_WOT, OF_WOT + 8192, KT, wot.rearrange("(k p) e -> k p e", p=128))
        B.emit("g", lambda g, d=d, s=s0: g.dma_start(out=d, in_=s), inc="DFN", inc_by=16)
        d, s0 = dma3(wg_sb, 0, KT * EXP, KT, wg)
        B.emit("g", lambda g, d=d, s=s0: g.dma_start(out=d, in_=s), inc="DFN", inc_by=16)
        d, s0 = dma3(msel_sb, 0, 2 * 512, 2, msel.rearrange("(nt p) e -> nt p e", p=128))
        B.emit("g", lambda g, d=d, s=s0: g.dma_start(out=d, in_=s), inc="DFN", inc_by=16)
        m_ffnw = B.mark("DFN")

        B.emit("s", lambda s: s.mul(ffnw[:, OF_LV:OF_LV + 2048], ffnw[:, OF_LV:OF_LV + 2048],
                                    0.5), waits=[("DFN", m_ffnw)], inc="S")
        m_lvscale = B.mark("S")
        B.emit("v", lambda v: v.tensor_scalar_mul(wob_sb[:], wob_sb[:], 0.5),
               waits=[("DX", m_x[0][0])], inc="V")
        B.emit("v", lambda v: v.tensor_copy(bgb[:], bg_sb[:]), waits=[("DX", m_x[0][0])], inc="V")
        m_pro_v = B.mark("V")

        state = {"wA": None, "wB": None}
        mw1 = {}
        mw2 = {}

        def load_w1(e, wait_T):
            if state["wA"] == e:
                return
            d = wA[:].rearrange("p (k h) -> p k h", k=KT)
            s0 = w1[e].rearrange("(k p) h -> k p h", p=128).transpose([1, 0, 2])
            B.emit("g", lambda g, d=d, s=s0: g.dma_start(out=d, in_=s),
                   waits=[("T", wait_T)], inc="DW1", inc_by=16)
            state["wA"] = e
            mw1[e] = B.mark("DW1")

        def load_w2(e, wait_T):
            if state["wB"] == e:
                return
            d = wB[:].rearrange("p (q dd) -> p q dd", q=HT)
            s0 = w2[e].rearrange("(q p) dd -> q p dd", p=128).transpose([1, 0, 2])
            B.emit("g", lambda g, d=d, s=s0: g.dma_start(out=d, in_=s),
                   waits=[("T", wait_T)], inc="DW2", inc_by=16)
            state["wB"] = e
            mw2[e] = B.mark("DW2")

        load_w1(0, 0)
        load_w2(0, 0)

        marks = {
            "xbln_free_T": 0,
            "arena_free_T": 0,
            "rep_free_V": 0,
        }
        xmark = {(k, s): 0 for k in range(KT) for s in range(2)}
        tmark = {0: 0, 1: 0}
        castmark = {(k, s): ("S", 0) for k in range(KT) for s in range(2)}
        ln_sqfree = {j: 0 for j in range(8)}
        ln_rowfree = {"m": []}
        ln_rb16free = {"m": 0}
        ln_repbfree = {"m": 0}
        ln_normbfree = {"m": 0}
        ffn_state = {"zq_free_T": {i: 0 for i in range(4)},
                     "sca_free_V": {0: 0}, "sct_free_V": {0: 0},
                     "sce_free_S": 0, "lpT_free_T": 0}

        def make_LN(c, dep, cast_in=False):
            mxln = [None, None]
            msub_ks = [None, None]

            def rb(c0, c1):
                return rowbuf[0:1, c0:c1]

            def half(s):
                b0, b1 = (4, 5) if s == 0 else (6, 7)
                if cast_in:
                    for k in range(KT):
                        w = [("T", marks["xbln_free_T"]), dep(k, s),
                             ("V", xmark[(k, s)])]
                        B.emit("s", lambda ss, k=k, s=s: ss.copy(
                            xb_ap(k, s * S, S), x_ap(k, s * S, S)), waits=w, inc="S")
                        castmark[(k, s)] = ("S", B.mark("S"))
                for k in range(KT):
                    w = [castmark[(k, s)]]
                    if k == 0:
                        w.append(bankw(b0))
                    B.emit("t", lambda t, s=s, k=k, b0=b0: t.matmul(
                        P[b0][0:1, :], onesc_sb[:], xb_ap(k, s * S, S),
                        start=(k == 0), stop=(k == KT - 1)),
                        waits=w, inc="T" if k == KT - 1 else None)
                msum = B.mark("T")
                for k in range(KT):
                    j = k % 4
                    gw = [castmark[(k, s)], ("T", ln_sqfree[j])]
                    B.emit("v", lambda v, k=k, s=s, j=j: v.tensor_mul(
                        sqb[:, j * S:(j + 1) * S], xb_ap(k, s * S, S),
                        xb_ap(k, s * S, S)), waits=gw, inc="V")
                    msqk = B.mark("V")
                    w = [("V", msqk)]
                    if k == 0:
                        w.append(bankw(b1))
                    B.emit("t", lambda t, k=k, j=j, b1=b1: t.matmul(
                        P[b1][0:1, :], onesc_sb[:], sqb[:, j * S:(j + 1) * S],
                        start=(k == 0), stop=(k == KT - 1)),
                        waits=w, inc="T")
                    ln_sqfree[j] = B.mark("T")
                msumsq = B.mark("T")
                B.emit("s", lambda ss, b0=b0: ss.activation(
                    rb(0, 512), P[b0][0:1, :], AF.Square, scale=1.0 / D),
                    waits=[("T", msum)] + ln_rowfree["m"], inc="S")
                mm2 = B.mark("S")
                B.emit("v", lambda v, b1=b1: v.scalar_tensor_tensor(
                    rb(512, 1024), P[b1][0:1, :], 1.0 / D, rb(0, 512),
                    ALU.mult, ALU.subtract),
                    waits=[("T", msumsq), ("S", mm2)] + ln_rowfree["m"], inc="V")
                mvar = B.mark("V")
                B.bank[b1] = ("V", mvar)
                B.emit("s", lambda ss, b0=b0: ss.mul(
                    rowb16[0:1, 0:512], P[b0][0:1, :], 1.0 / D),
                    waits=[("T", msum), ("T", ln_rb16free["m"])], inc="S")
                mmean = B.mark("S")
                B.bank[b0] = ("S", mmean)
                B.emit("s", lambda ss: ss.activation(
                    rb(0, 512), rb(512, 1024), AF.Ln, bias=epsc_sb[0:1, 0:1]),
                    waits=[("V", mvar)], inc="S")
                mlnv = B.mark("S")
                B.emit("s", lambda ss: ss.activation(
                    rowb16[0:1, 512:1024], rb(0, 512), AF.Exp, scale=-0.5),
                    waits=[("S", mlnv), ("T", ln_rb16free["m"])], inc="S")
                mrstd = B.mark("S")
                ln_rowfree["m"] = [("S", mrstd), ("V", mvar)]
                B.emit("t", lambda t, b0=b0: t.matmul(
                    P[b0][:, :], onesr_sb[:], rowb16[0:1, 512:1024],
                    start=True, stop=True), waits=[("S", mrstd), bankw(b0)])
                B.emit("t", lambda t, b1=b1: t.matmul(
                    P[b1][:, :], onesr_sb[:], rowb16[0:1, 0:512],
                    start=True, stop=True), waits=[("S", mmean), bankw(b1)], inc="T")
                mrep = B.mark("T")
                ln_rb16free["m"] = mrep
                B.emit("s", lambda ss, b1=b1: ss.copy(repb[:, 512:1024], P[b1][:, :]),
                       waits=[("T", mrep), ("V", ln_repbfree["m"])], inc="S")
                mevm = B.mark("S")
                B.bank[b1] = ("S", mevm)
                B.emit("s", lambda ss, b0=b0: ss.copy(repb[:, 0:512], P[b0][:, :]),
                       waits=[("T", mrep), ("V", ln_repbfree["m"])], inc="S")
                mevr = B.mark("S")
                B.bank[b0] = ("S", mevr)
                seq = []
                for k in range(KT):
                    seq.append(("sub", k))
                    seq.append(("mul", k))
                mmulk = {}
                msubk = {}
                for kind, k in seq:
                    if kind == "sub":
                        w = [("S", mevm), castmark[(k, s)]]
                        if k >= 2:
                            w.append(("V", mmulk[k - 2]))
                        else:
                            w.append(("V", ln_normbfree["m"]))
                        B.emit("v", lambda v, k=k, s=s: v.tensor_tensor(
                            normb[:, (k % 2) * S:(k % 2 + 1) * S], xb_ap(k, s * S, S),
                            repb[:, 512:1024], ALU.subtract), waits=w, inc="V")
                        msubk[k] = B.mark("V")
                    else:
                        B.emit("v", lambda v, k=k, s=s: v.tensor_tensor(
                            xb_ap(k, s * S, S), normb[:, (k % 2) * S:(k % 2 + 1) * S],
                            repb[:, 0:512], ALU.mult),
                            waits=[("V", msubk[k]), ("S", mevr)], inc="V")
                        mmulk[k] = B.mark("V")
                mxln[s] = B.mark("V")
                msub_ks[s] = [mmulk[k] for k in range(KT)]
                ln_repbfree["m"] = mxln[s]
                ln_normbfree["m"] = mxln[s]

            return (lambda: half(0)), (lambda: half(1)), mxln, msub_ks

        def make_FFN(c, cast=False, store=False):
            QB = {0: (0, 1), 1: (2, 3)}
            st = ffn_state
            H = {0: {"mfe": {}}, 1: {"mfe": {}}}
            MS = {}
            done = {"q0": False}

            def q_group(s, mids=None):
                qb = QB[s]
                mq = []
                mqe = []
                for et in range(EXP):
                    if mids and et in mids:
                        mids[et]()
                    for k in range(KT):
                        w = [("V", MS[s][k]), ("DFN", m_ffnw)]
                        if k == 0:
                            if et >= 2:
                                w.append(("S", mqe[et - 2]))
                            else:
                                w.append(bankw(qb[et % 2]))
                        B.emit("t", lambda t, et=et, k=k, s=s, qb=qb: t.matmul(
                            P[qb[et % 2]][:, :], wqt_ap(k, et), xb_ap(k, s * S, S),
                            start=(k == 0), stop=(k == KT - 1)),
                            waits=w, inc="T" if k == KT - 1 else None)
                    mq.append(B.mark("T"))
                    ew = [("T", mq[et]), ("T", marks["arena_free_T"])]
                    if s == 1:
                        ew.append(("T", H[0]["m_z_end"]))
                    B.emit("s", lambda ss, et=et, qb=qb: ss.copy(
                        hp_ap(et), P[qb[et % 2]][:, :]), waits=ew, inc="S")
                    mqe.append(B.mark("S"))
                B.bank[qb[0]] = ("S", mqe[6])
                B.bank[qb[1]] = ("S", mqe[7])
                H[s]["mqe"] = mqe

            def z_group(s):
                mz = []
                for nt in range(2):
                    for et in range(EXP):
                        w = [("S", H[s]["mqe"][et])]
                        if et == 0:
                            w.append(bankw(4 + nt))
                        B.emit("t", lambda t, nt=nt, et=et: t.matmul(
                            P[4 + nt][:, :], rtt_ap(et, nt), hp_ap(et),
                            start=(et == 0), stop=(et == EXP - 1)),
                            waits=w, inc="T" if et == EXP - 1 else None)
                    mz.append(B.mark("T"))
                H[s]["mz"] = mz
                H[s]["m_z_end"] = B.mark("T")

            def schain_group(s):
                mcp = {}
                mq2 = {}
                for nt in range(2):
                    B.emit("s", lambda ss, nt=nt: ss.copy(zq_z(nt), P[4 + nt][:, :]),
                           waits=[("T", H[s]["mz"][nt]),
                                  ("T", st["zq_free_T"][nt])], inc="S")
                    mcp[nt] = B.mark("S")
                    B.emit("s", lambda ss, nt=nt: ss.activation(
                        sca(nt), P[4 + nt][:, :], AF.Abs),
                        waits=[("S", mcp[nt]), ("V", st["sca_free_V"][0])], inc="S")
                    mab = B.mark("S")
                    B.bank[4 + nt] = ("S", mab)
                    B.emit("s", lambda ss, nt=nt: ss.activation(
                        sceb[:, :], sca(nt), AF.Exp, scale=-1.0),
                        waits=[("S", mab), ("S", st["sce_free_S"])], inc="S")
                    mex = B.mark("S")
                    B.emit("s", lambda ss, nt=nt: ss.activation(
                        sct(nt), sceb[:, :], AF.Ln, bias=1.0),
                        waits=[("S", mex), ("V", st["sct_free_V"][0])], inc="S")
                    mln = B.mark("S")
                    st["sce_free_S"] = mln
                    B.emit("v", lambda v, nt=nt: v.scalar_tensor_tensor(
                        zq_q(nt), sca(nt), 0.5, sct(nt), ALU.mult, ALU.add),
                        waits=[("S", mln), ("T", st["zq_free_T"][2 + nt])], inc="V")
                    mq2[nt] = B.mark("V")
                    st["sca_free_V"][0] = mq2[nt]
                    st["sct_free_V"][0] = mq2[nt]
                H[s]["mcp"] = mcp
                H[s]["mq2"] = mq2

            def ll_group(s):
                mll = []
                for lt in range(2):
                    ops = [("z", 0), ("z", 1), ("q", 0), ("q", 1)]
                    for i, (kind, nt) in enumerate(ops):
                        dep = (("S", H[s]["mcp"][nt]) if kind == "z"
                               else ("V", H[s]["mq2"][nt]))
                        w = [dep, ("DFN", m_ffnw)]
                        if i == 0:
                            w.append(bankw(6 + lt))
                        B.emit("t", lambda t, kind=kind, nt=nt, lt=lt, i=i: t.matmul(
                            P[6 + lt][:, :], msel_ap(nt, kind, lt),
                            (zq_z(nt) if kind == "z" else zq_q(nt)),
                            start=(i == 0), stop=(i == 3)),
                            waits=w, inc="T" if i == 3 else None)
                    mll.append(B.mark("T"))
                mend = B.mark("T")
                for i in range(4):
                    st["zq_free_T"][i] = mend
                mexp = []
                for lt in range(2):
                    B.emit("s", lambda ss, lt=lt: ss.activation(
                        lpT[:, lt * S:(lt + 1) * S], P[6 + lt][:, :], AF.Exp),
                        waits=[("T", mll[lt]), ("T", st["lpT_free_T"])], inc="S")
                    mexp.append(B.mark("S"))
                    B.bank[6 + lt] = ("S", mexp[lt])
                H[s]["mexp"] = mexp

            def ret_group(s):
                qb = QB[s]
                slot = ret_ap if s == 0 else hp_ap
                mret = []
                mrete = []
                for dt in range(KT):
                    for l in range(2):
                        w = [("S", H[s]["mexp"][l]), ("S", m_lvscale)]
                        if l == 0:
                            if dt >= 2:
                                w.append(("S", mrete[dt - 2]))
                            else:
                                w.append(bankw(qb[dt % 2]))
                        B.emit("t", lambda t, dt=dt, l=l, qb=qb: t.matmul(
                            P[qb[dt % 2]][:, :], lv_ap(l, dt), lpT[:, l * S:(l + 1) * S],
                            start=(l == 0), stop=(l == 1)),
                            waits=w, inc="T" if l == 1 else None)
                    mret.append(B.mark("T"))
                    ew = [("T", mret[dt])]
                    if s == 0:
                        ew.append(("T", marks["arena_free_T"]))
                    else:
                        ew.append(("T", H[1]["m_z_end"]))
                    B.emit("s", lambda ss, dt=dt, qb=qb, slot=slot: ss.copy(
                        slot(dt), P[qb[dt % 2]][:, :]), waits=ew, inc="S")
                    mrete.append(B.mark("S"))
                B.bank[qb[0]] = ("S", mrete[6])
                B.bank[qb[1]] = ("S", mrete[7])
                H[s]["mrete"] = mrete
                H[s]["slot"] = slot
                st["lpT_free_T"] = B.mark("T")

            def f_group(s, dts):
                qb = QB[s]
                slot = H[s]["slot"]
                mfe = H[s]["mfe"]
                for dt in dts:
                    for k in range(KT):
                        w = [("S", H[s]["mrete"][k])]
                        if k == 0:
                            if dt >= 2:
                                w.append(("V", mfe[dt - 2]))
                            else:
                                w.append(bankw(qb[dt % 2]))
                        B.emit("t", lambda t, dt=dt, k=k, qb=qb, slot=slot: t.matmul(
                            P[qb[dt % 2]][:, :], wot_ap(k, dt), slot(k),
                            start=(k == 0), stop=(k == KT - 1)),
                            waits=w, inc="T" if k == KT - 1 else None)
                    mfdt = B.mark("T")
                    B.emit("v", lambda v, dt=dt, s=s, qb=qb: v.scalar_tensor_tensor(
                        x_ap(dt, s * S, S), P[qb[dt % 2]][:, :], wob_sb[:, dt:dt + 1],
                        x_ap(dt, s * S, S), ALU.add, ALU.add),
                        waits=[("T", mfdt), ("V", m_pro_v), ("V", xmark[(dt, s)])],
                        inc="V")
                    mfe[dt] = B.mark("V")
                    xmark[(dt, s)] = B.mark("V")
                    if cast:
                        B.emit("s", lambda ss, dt=dt, s=s: ss.copy(
                            xb_ap(dt, s * S, S), x_ap(dt, s * S, S)),
                            waits=[("V", xmark[(dt, s)])], inc="S")
                        castmark[(dt, s)] = ("S", B.mark("S"))
                    if store:
                        dst = out[dt * 128:(dt + 1) * 128,
                                  c * CH + s * S:c * CH + s * S + S]
                        src = x_ap(dt, s * S, S)
                        B.emit("sy", lambda sy, dst=dst, src=src: sy.dma_start(
                            out=dst, in_=src), waits=[("V", xmark[(dt, s)])],
                            inc="DO%d" % s, inc_by=16)
                if store and dts[-1] == KT - 1:
                    mo = B.mark("DO%d" % s)
                    if c + 1 < nch:
                        for k in range(KT):
                            s0_ = xT[k * 128:(k + 1) * 128,
                                     (c + 1) * CH + s * S:(c + 1) * CH + s * S + S]
                            B.emit("sy", lambda sy, k=k, s=s, s0_=s0_: sy.dma_start(
                                out=x_ap(k, s * S, S), in_=s0_),
                                waits=[("DO%d" % s, mo)], inc="DR%d" % s, inc_by=16)
                        m_x[c + 1][s] = B.mark("DR%d" % s)
                if dts[-1] == KT - 1:
                    B.bank[qb[0]] = ("V", mfe[6])
                    B.bank[qb[1]] = ("V", mfe[7])

            if store and c + 1 < nch:
                m_x[c + 1] = [None, None]

            def q0(msk0):
                MS[0] = msk0
                q_group(0)
                done["q0"] = True

            def rest(msk, f1_hook=None):
                MS[0], MS[1] = msk[0], msk[1]
                if not done["q0"]:
                    q_group(0)
                z_group(0)
                schain_group(0)
                q_group(1, mids={2: lambda: ll_group(0)})
                z_group(1)
                schain_group(1)
                ret_group(0)
                f_group(0, [0, 1, 2, 3])
                ll_group(1)
                f_group(0, [4, 5, 6, 7])
                ret_group(1)
                f_group(1, [0, 1, 2, 3])
                if f1_hook is not None:
                    f1_hook()
                f_group(1, [4, 5, 6, 7])
                marks["arena_free_T"] = B.mark("T")
                marks["xbln_free_T"] = B.mark("T")
            return q0, rest

        def make_MASK(c, mstop="full"):
            mlg = []

            def gate_groups(cb8s, msk_s):
                for cb8 in cb8s:
                    for k in range(KT):
                        w = [("V", msk_s[k]), ("DFN", m_ffnw)]
                        if k == 0 and cb8 == 0:
                            w.append(bankw(0))
                        B.emit("t", lambda t, cb8=cb8, k=k: t.matmul(
                            P[0][:, cb8 * EXP:(cb8 + 1) * EXP],
                            xbln[:, k * CH + cb8 * 128:k * CH + cb8 * 128 + 128],
                            wg_sb[:, k * EXP:(k + 1) * EXP], start=(k == 0), stop=False),
                            waits=w)
                    B.emit("t", lambda t, cb8=cb8: t.matmul(
                        P[0][:, cb8 * EXP:(cb8 + 1) * EXP], onesr_sb[:], bgb[:],
                        start=False, stop=True), waits=[("V", m_pro_v)], inc="T")
                    mlg.append(B.mark("T"))

            def gate0(msk0):
                gate_groups(range(4), msk0)

            def rest(msk):
                if not mlg:
                    gate_groups(range(4), msk[0])
                gate_groups(range(4, 8), msk[1])
                return softmax_part()

            def softmax_part():
                B.emit("s", lambda ss: ss.copy(smb[:, 0:8 * EXP], P[0][:, 0:8 * EXP]),
                       waits=[("T", mlg[7])], inc="S")
                mlge = B.mark("S")
                B.bank[0] = ("S", mlge)
                if mstop == "lg":
                    return None
                def g3(buf):
                    return buf[:, 0:8 * EXP].rearrange("p (g e) -> p g e", g=8)

                def r3(buf):
                    return buf[:, 0:8].unsqueeze(2)

                def r3b(buf):
                    return buf[:, 0:8].unsqueeze(2).broadcast_to([128, 8, EXP])
                B.emit("v", lambda v: v.reduce_max(r3(rmeb), g3(smb), AX.X),
                       waits=[("S", mlge)], inc="V")
                mrm = B.mark("V")
                B.emit("v", lambda v: v.tensor_tensor(
                    g3(smeb), g3(smb), r3b(rmeb), ALU.subtract),
                    waits=[("V", mrm)], inc="V")
                msub = B.mark("V")
                if mstop == "sub":
                    return None
                B.emit("s", lambda ss: ss.activation(smb[:], smeb[:], AF.Exp),
                       waits=[("V", msub)], inc="S")
                mexp = B.mark("S")
                if mstop == "exp":
                    return None
                B.emit("v", lambda v: v.reduce_sum(r3(rsbb), g3(smb), AX.X),
                       waits=[("S", mexp)], inc="V")
                mrs = B.mark("V")
                B.emit("v", lambda v: v.reciprocal(rsbb[:, 0:8], rsbb[:, 0:8]),
                       waits=[("V", mrs)], inc="V")
                mrcp = B.mark("V")
                if mstop == "recip":
                    return None
                B.emit("v", lambda v: v.tensor_tensor(
                    g3(smeb), g3(smb), r3b(rsbb), ALU.mult),
                    waits=[("V", mrcp)], inc="V")
                mgw = B.mark("V")
                B.emit("v", lambda v: v.reduce_max(r3(rmeb), g3(smeb), AX.X),
                       waits=[("V", mgw)], inc="V")
                mrm2 = B.mark("V")
                B.emit("v", lambda v: v.tensor_tensor(
                    g3(maskb), g3(smeb), r3b(rmeb), ALU.is_ge),
                    waits=[("V", mrm2)], inc="V")
                mge = B.mark("V")
                B.emit("v", lambda v: v.tensor_mul(maskb[:], maskb[:], smeb[:]),
                       waits=[("V", mge)], inc="V")
                mmask = B.mark("V")
                return mmask
            return gate0, rest

        def emit_MASK_finish(mmask):
            for cb8 in range(8):
                bank = 4 if cb8 < 4 else 5
                cc = (cb8 % 4) * 128
                w = [("V", mmask)]
                if cb8 % 4 == 0:
                    w.append(bankw(bank))
                B.emit("t", lambda t, cb8=cb8, bank=bank, cc=cc: t.transpose(
                    P[bank][0:EXP, cc:cc + 128], maskb[:, cb8 * EXP:(cb8 + 1) * EXP], idf_sb[:]),
                    waits=w, inc="T" if cb8 == 7 else None)
            mmt = B.mark("T")
            mallv = B.mark("V")
            B.emit("v", lambda v: v.tensor_copy(tmpf[0:EXP, 0:S], P[4][0:EXP, :]),
                   waits=[("T", mmt), ("V", mallv)], inc="V")
            mtc0 = B.mark("V")
            B.emit("v", lambda v: v.tensor_copy(tmpf[0:EXP, S:2 * S], P[5][0:EXP, :]), inc="V")
            mtc1 = B.mark("V")
            B.emit("v", lambda v: v.tensor_copy(mTb[0:EXP, 0:S], tmpf[0:EXP, 0:S]),
                   waits=[("V", mtc0)], inc="V")
            tmark[0] = B.mark("V")
            B.emit("v", lambda v: v.tensor_copy(mTb[0:EXP, S:CH], tmpf[0:EXP, S:2 * S]),
                   waits=[("V", mtc1)], inc="V")
            tmark[1] = B.mark("V")
            mmaskT = B.mark("V")
            B.bank[4] = ("V", mmaskT)
            B.bank[5] = ("V", mmaskT)
            return mmaskT

        def make_ATTN(c):
            order = list(range(EXP)) if c % 2 == 0 else list(range(EXP - 1, -1, -1))
            ST = {"mmaskT": None, "mrepe": None}
            MS = {}

            def emit_mm1(e, s):
                mm1 = []
                mhe = []
                for h in range(HT):
                    for k in range(KT):
                        w = [("DW1", mw1[e]), ("V", MS[s][k])]
                        if k == 0:
                            if h >= 2:
                                w.append(("S", mhe[h - 2]))
                            else:
                                w.append(bankw(h % 2))
                        B.emit("t", lambda t, h=h, k=k, s=s: t.matmul(
                            P[h % 2][:, :], w1_ap(k, h), xb_ap(k, s * S, S),
                            start=(k == 0), stop=(k == KT - 1)),
                            waits=w, inc="T" if k == KT - 1 else None)
                    mm1.append(B.mark("T"))
                    B.emit("s", lambda ss, h=h, e=e: ss.activation(
                        hp_ap(h), P[h % 2][:, :], AF.Relu,
                        bias=b1t_sb[:, e * HT + h:e * HT + h + 1]),
                        waits=[("T", mm1[h]), ("T", marks["arena_free_T"])], inc="S")
                    mhe.append(B.mark("S"))
                B.bank[0] = ("S", mhe[14])
                B.bank[1] = ("S", mhe[15])
                return mhe, B.mark("T")

            def emit_rep(e):
                B.emit("t", lambda t, e=e: t.matmul(
                    P[4][:, :], sel_sb[:, e * 128:(e + 1) * 128], mTb[0:EXP, 0:S],
                    start=True, stop=True),
                    waits=[("V", ST["mmaskT"]), ("V", marks["rep_free_V"]), bankw(4)])
                B.emit("t", lambda t, e=e: t.matmul(
                    P[5][:, :], sel_sb[:, e * 128:(e + 1) * 128], mTb[0:EXP, S:CH],
                    start=True, stop=True), waits=[bankw(5)], inc="T")
                mrepm = B.mark("T")
                B.emit("v", lambda v: v.tensor_copy(rep[:, 0:S], P[4][:, :]),
                       waits=[("T", mrepm), ("V", marks["rep_free_V"])], inc="V")
                B.emit("v", lambda v: v.tensor_copy(rep[:, S:CH], P[5][:, :]), inc="V")
                mrepe = B.mark("V")
                B.bank[4] = ("V", mrepe)
                B.bank[5] = ("V", mrepe)
                ST["mrepe"] = mrepe
                return mrepe

            def emit_mm2(e, s, mhe, midfns, last):
                banks = [2, 3, 6, 7] if midfns else [2, 3]
                nb = len(banks)
                mm2 = {}
                matt = {}
                pend = [None]
                pending = []

                def flush_one(dt):
                    bk = banks[dt % nb]
                    w = [("T", mm2[dt]), ("V", ST["mrepe"]), ("V", tmark[dt % 2])]
                    B.emit("v", lambda v, dt=dt, e=e, s=s, bk=bk: v.scalar_tensor_tensor(
                        tmpf[:, (dt % 2) * S:(dt % 2 + 1) * S], P[bk][:, :],
                        b2c_sb[:, e * KT + dt:e * KT + dt + 1], rep[:, s * S:(s + 1) * S],
                        ALU.add, ALU.mult),
                        waits=w, inc="V")
                    matt[dt] = B.mark("V")
                    if pend[0] is not None:
                        pdt, pfn = pend[0]
                        B.emit("v", pfn, waits=[("V", matt[pdt]),
                                                ("V", xmark[(pdt, s)])], inc="V")
                        xmark[(pdt, s)] = B.mark("V")
                        tmark[pdt % 2] = B.mark("V")
                        if last:
                            B.emit("s", lambda ss, pdt=pdt, s=s: ss.copy(
                                xb_ap(pdt, s * S, S), x_ap(pdt, s * S, S)),
                                waits=[("V", xmark[(pdt, s)])], inc="S")
                            castmark[(pdt, s)] = ("S", B.mark("S"))
                    pend[0] = (dt, (lambda v, dt=dt, s=s: v.tensor_add(
                        x_ap(dt, s * S, S), x_ap(dt, s * S, S),
                        tmpf[:, (dt % 2) * S:(dt % 2 + 1) * S])))

                for dt in range(KT):
                    if midfns and dt in midfns:
                        midfns[dt]()
                    if ST["mrepe"] is not None:
                        while pending:
                            flush_one(pending.pop(0))
                    bk = banks[dt % nb]
                    for q in range(HT):
                        w = [("DW2", mw2[e]), ("S", mhe[q])]
                        if q == 0:
                            if dt >= nb:
                                w.append(("V", matt[dt - nb]))
                            else:
                                w.append(bankw(bk))
                        B.emit("t", lambda t, dt=dt, q=q, bk=bk: t.matmul(
                            P[bk][:, :], w2_ap(q, dt), hp_ap(q),
                            start=(q == 0), stop=(q == HT - 1)),
                            waits=w, inc="T" if q == HT - 1 else None)
                    mm2[dt] = B.mark("T")
                    pending.append(dt)
                    if ST["mrepe"] is not None:
                        while pending:
                            flush_one(pending.pop(0))
                while pending:
                    flush_one(pending.pop(0))
                pdt, pfn = pend[0]
                B.emit("v", pfn, waits=[("V", matt[pdt]), ("V", xmark[(pdt, s)])], inc="V")
                xmark[(pdt, s)] = B.mark("V")
                tmark[pdt % 2] = B.mark("V")
                if last:
                    B.emit("s", lambda ss, pdt=pdt, s=s: ss.copy(
                        xb_ap(pdt, s * S, S), x_ap(pdt, s * S, S)),
                        waits=[("V", xmark[(pdt, s)])], inc="S")
                    castmark[(pdt, s)] = ("S", B.mark("S"))
                for i, b in enumerate(banks):
                    B.bank[b] = ("V", matt[KT - nb + i])

            def mm1_first(msk0):
                MS[0] = msk0
                ST["first"] = emit_mm1(order[0], 0)

            def rest(msk, mask_fn, ln3_half0=None):
                MS[0], MS[1] = msk[0], msk[1]
                for ei, e in enumerate(order):
                    last = (ei == EXP - 1)
                    for s in range(2):
                        if ei == 0 and s == 0:
                            mhe, m_mm1_end = ST["first"]
                            mk = {"mmask": None}

                            def mid_gates():
                                mk["mmask"] = mask_fn()

                            def mid_finish():
                                ST["mmaskT"] = emit_MASK_finish(mk["mmask"])
                                emit_rep(e)
                            ST["mrepe"] = None
                            emit_mm2(e, s, mhe, {2: mid_gates, 4: mid_finish}, last)
                        else:
                            mhe, m_mm1_end = emit_mm1(e, s)
                            if s == 0:
                                emit_rep(e)
                            if s == 1 and last and ln3_half0 is not None:
                                ln3_half0()
                            if s == 1 and ei < EXP - 1:
                                load_w1(order[ei + 1], m_mm1_end)
                            emit_mm2(e, s, mhe, None, last)
                        marks["arena_free_T"] = B.mark("T")
                        if s == 1 and ei < EXP - 1:
                            load_w2(order[ei + 1], marks["arena_free_T"])
                    marks["rep_free_V"] = B.mark("V")
                marks["xbln_free_T"] = B.mark("T")
            return mm1_first, rest

        def done(phase):
            order = ["ln1", "ffn1", "ln2", "mask", "attn", "ln3", "ffn2", "full"]
            return order.index(stop_after) < order.index(phase)

        def lnd(cc):
            def dep(k, s, cc=cc):
                if cc == 0:
                    return ("DX", m_x[cc][s])
                return ("DR%d" % s, m_x[cc][s])
            return dep

        ln1_h0, ln1_h1, ln1_mxln, ln1_msk = make_LN(0, lnd(0), cast_in=True)
        ln1_h0()
        for c in range(nch):
            ffn1_q0, ffn1_rest = make_FFN(c, cast=True)
            ffn1_q0(ln1_msk[0])
            ln1_h1()
            if done("ffn1"):
                break
            ln2_h0, ln2_h1, ln2_mxln, ln2_msk = make_LN(c, None)
            ffn1_rest(ln1_msk, f1_hook=ln2_h0)
            if done("ln2"):
                break
            attn_mm1, attn_rest = make_ATTN(c)
            attn_mm1(ln2_msk[0])
            ln2_h1()
            if done("mask"):
                break
            _, mask_rest = make_MASK(c, mstop=mask_stop)
            ln3_h0, ln3_h1, ln3_mxln, ln3_msk = make_LN(c, None)
            attn_rest(ln2_msk, mask_fn=lambda: mask_rest(ln2_msk),
                      ln3_half0=ln3_h0)
            if done("attn") or done("ln3"):
                break
            ffn2_q0, ffn2_rest = make_FFN(c, cast=False, store=True)
            ffn2_q0(ln3_msk[0])
            ln3_h1()
            if done("ffn2"):
                break
            if c + 1 < nch:
                ln1_h0, ln1_h1, ln1_mxln, ln1_msk = make_LN(
                    c + 1, lnd(c + 1), cast_in=True)
                ffn2_rest(ln3_msk, f1_hook=ln1_h0)
            else:
                ffn2_rest(ln3_msk)

        if stop_after != "full":
            wd = [("V", B.mark("V")), ("S", B.mark("S")), ("T", B.mark("T"))]
            dst = out.rearrange("(k p) t -> k p t", p=128)[:, :, 0:CH].transpose([1, 0, 2])
            src = x_sb[:].rearrange("p (k j) -> p k j", k=KT)
            B.emit("sy", lambda sy, dst=dst, src=src: sy.dma_start(out=dst, in_=src),
                   waits=wd, inc="DX", inc_by=16)
            dst = out.rearrange("(k p) t -> k p t", p=128)[:, :, CH:2 * CH].transpose([1, 0, 2])
            src = xbln[:].rearrange("p (k j) -> p k j", k=KT)
            B.emit("g", lambda g, dst=dst, src=src: g.dma_start(out=dst, in_=src),
                   waits=wd, inc="DX", inc_by=16)

        m_final = B.mark("DX")

        m_do0 = B.mark("DO0")
        m_do1 = B.mark("DO1")

        @block.sync
        def _(sy):
            B.play("sy", sy, sems)
            sy.wait_ge(sem_DX, m_final)
            if m_do0:
                sy.wait_ge(sem_DO0, m_do0)
            if m_do1:
                sy.wait_ge(sem_DO1, m_do1)

        @block.gpsimd
        def _(g):
            B.play("g", g, sems)

        @block.tensor
        def _(t):
            B.play("t", t, sems)

        @block.vector
        def _(v):
            B.play("v", v, sems)

        @block.scalar
        def _(s):
            B.play("s", s, sems)

    return nc


def _host_inputs(L0_attn_Wg, L0_attn_bg, L0_attn_W1, L0_attn_b1, L0_attn_W2, L0_attn_b2,
                 L0_ffn_Wq, L0_ffn_routers, L0_ffn_leaf_values, L0_ffn_Wout_w, L0_ffn_Wout_b):
    f = np.float32
    shared = {
        "w1": np.ascontiguousarray(L0_attn_W1, dtype=f),
        "w2": np.ascontiguousarray(L0_attn_W2, dtype=f),
        "b1t": np.ascontiguousarray(np.asarray(L0_attn_b1, dtype=f).reshape(EXP, HT, 128)
                                    .transpose(0, 2, 1)),
        "b2c": np.ascontiguousarray(np.asarray(L0_attn_b2, dtype=f).reshape(EXP, KT, 128)
                                    .transpose(0, 2, 1)),
        "wg": np.ascontiguousarray(np.asarray(L0_attn_Wg, dtype=f).reshape(KT, 128, EXP)),
        "bg": np.ascontiguousarray(np.asarray(L0_attn_bg, dtype=f).reshape(1, EXP)),
        "wqt": np.ascontiguousarray(np.asarray(L0_ffn_Wq, dtype=f).T),
        "rtt": np.ascontiguousarray(np.concatenate(
            [np.asarray(L0_ffn_routers, dtype=f).T, np.zeros((D, 1), dtype=f)], axis=1)),
        "lv": np.ascontiguousarray(L0_ffn_leaf_values, dtype=f),
        "wot": np.ascontiguousarray(np.asarray(L0_ffn_Wout_w, dtype=f).T),
        "wob": np.ascontiguousarray(np.asarray(L0_ffn_Wout_b, dtype=f).reshape(KT, 128).T),
        "idf": np.eye(128, dtype=f),
        "onesc": np.ones((128, 1), dtype=ml_dtypes.bfloat16),
        "epsc": np.full((1, 1), EPS, dtype=f),
        "onesr": np.ones((1, 128), dtype=ml_dtypes.bfloat16),
    }
    sel = np.zeros((EXP, EXP * 128), dtype=ml_dtypes.bfloat16)
    for e in range(EXP):
        sel[e, e * 128:(e + 1) * 128] = 1
    shared["seli"] = sel
    ms = np.zeros((256, 512), dtype=np.float32)
    for m in range(256):
        for lev in range(8):
            j = (1 << lev) - 1 + (m >> (8 - lev))
            right = (m >> (7 - lev)) & 1
            ms[j, m] = 0.5 if right else -0.5
            ms[j, 256 + m] = -1.0
    shared["msel"] = ms.astype(ml_dtypes.bfloat16)
    return shared


def kernel(L0_attn_x, L0_attn_Wg, L0_attn_bg, L0_attn_W1, L0_attn_b1, L0_attn_W2, L0_attn_b2,
           L0_ffn_Wq, L0_ffn_routers, L0_ffn_leaf_keys, L0_ffn_leaf_values,
           L0_ffn_Wout_w, L0_ffn_Wout_b):
    x = np.asarray(L0_attn_x, dtype=np.float32)
    shared = _host_inputs(L0_attn_Wg, L0_attn_bg, L0_attn_W1, L0_attn_b1, L0_attn_W2,
                          L0_attn_b2, L0_ffn_Wq, L0_ffn_routers, L0_ffn_leaf_values,
                          L0_ffn_Wout_w, L0_ffn_Wout_b)
    nc = build_nc()
    in_maps = []
    for b in range(8):
        m = dict(shared)
        m["xT"] = np.ascontiguousarray(x[b].T)
        in_maps.append(m)
    res = run_bass_kernel_spmd(nc, in_maps, core_ids=list(range(8)))
    outs = [np.asarray(r["out"]).T for r in res.results]
    return np.ascontiguousarray(np.stack(outs, axis=0), dtype=np.float32)


# version 23 (final)
# speedup vs baseline: 1.1067x; 1.0048x over previous
import numpy as np
import ml_dtypes
import concourse.bass as bass
import concourse.mybir as mybir
from concourse.bass_utils import run_bass_kernel_spmd
from contextlib import ExitStack

F32 = mybir.dt.float32
BF16 = mybir.dt.bfloat16
I32 = mybir.dt.int32
AF = mybir.ActivationFunctionType
ALU = mybir.AluOpType
AX = mybir.AxisListType

D = 1024
TOK = 4096
CH = 1024
NCH = TOK // CH
S = 512
KT = 8
HT = 16
EXP = 8
EPS = 1e-5

OF_WQT = 0
OF_RTT = 8192
OF_LV = OF_RTT + 8 * 256
OF_WOT = OF_LV + 2 * 1024
FFNW_COLS = OF_WOT + 8192


class Sched:
    ENGS = ("t", "v", "s", "sy", "g")
    OWN = {"t": "T", "v": "V", "s": "S", "sy": None, "g": None}

    def __init__(self):
        self.q = {e: [] for e in self.ENGS}
        self.cnt = {"T": 0, "V": 0, "S": 0, "G": 0,
                    "DX": 0, "DFN": 0, "DW1": 0, "DW2": 0,
                    "DO0": 0, "DO1": 0, "DR0": 0, "DR1": 0}
        self.bank = {i: ("T", 0) for i in range(8)}

    def emit(self, eng, fn, waits=(), inc=None, inc_by=1):
        ws = tuple((sn, vv) for (sn, vv) in waits if vv)
        self.q[eng].append((ws, fn, inc, inc_by))
        if inc is not None:
            self.cnt[inc] += inc_by
        return self.cnt[inc] if inc is not None else None

    def mark(self, sem):
        return self.cnt[sem]

    def play(self, eng, engobj, sems):
        last = {}
        for waits, fn, inc, inc_by in self.q[eng]:
            for sem_name, val in waits:
                if val is None or val <= 0:
                    continue
                if last.get(sem_name, -1) >= val:
                    continue
                engobj.wait_ge(sems[sem_name], val)
                last[sem_name] = val
            inst = fn(engobj)
            if inc is not None:
                inst.then_inc(sems[inc], inc_by)


def build_nc(nch=NCH, stop_after="full", mask_stop="full"):
    nc = bass.Bass("TRN2", target_bir_lowering=False)

    xT = nc.dram_tensor("xT", [D, TOK], F32, kind="ExternalInput").ap()
    w1 = nc.dram_tensor("w1", [EXP, D, 2048], F32, kind="ExternalInput").ap()
    w2 = nc.dram_tensor("w2", [EXP, 2048, D], F32, kind="ExternalInput").ap()
    b1t = nc.dram_tensor("b1t", [EXP, 128, HT], F32, kind="ExternalInput").ap()
    b2c = nc.dram_tensor("b2c", [EXP, 128, KT], F32, kind="ExternalInput").ap()
    wg = nc.dram_tensor("wg", [KT, 128, EXP], F32, kind="ExternalInput").ap()
    bg = nc.dram_tensor("bg", [1, EXP], F32, kind="ExternalInput").ap()
    wqt = nc.dram_tensor("wqt", [D, D], F32, kind="ExternalInput").ap()
    rtt = nc.dram_tensor("rtt", [D, 256], F32, kind="ExternalInput").ap()
    msel = nc.dram_tensor("msel", [256, 512], BF16, kind="ExternalInput").ap()
    lv = nc.dram_tensor("lv", [256, D], F32, kind="ExternalInput").ap()
    wot = nc.dram_tensor("wot", [D, D], F32, kind="ExternalInput").ap()
    wob = nc.dram_tensor("wob", [128, KT], F32, kind="ExternalInput").ap()
    idf = nc.dram_tensor("idf", [128, 128], F32, kind="ExternalInput").ap()
    onesc = nc.dram_tensor("onesc", [128, 1], BF16, kind="ExternalInput").ap()
    epsc = nc.dram_tensor("epsc", [1, 1], F32, kind="ExternalInput").ap()
    onesr = nc.dram_tensor("onesr", [1, 128], BF16, kind="ExternalInput").ap()
    seli = nc.dram_tensor("seli", [EXP, EXP * 128], BF16, kind="ExternalInput").ap()
    out = nc.dram_tensor("out", [D, TOK], F32, kind="ExternalOutput").ap()

    B = Sched()

    with ExitStack() as stack:
        E = stack.enter_context
        x_sb = E(nc.sbuf_tensor("x_sb", [128, KT * CH], F32))
        xbln = E(nc.sbuf_tensor("xbln", [128, KT * CH], BF16))
        wA = E(nc.sbuf_tensor("wA", [128, KT * 2048], BF16))
        wB = E(nc.sbuf_tensor("wB", [128, HT * 1024], BF16))
        ffnw = E(nc.sbuf_tensor("ffnw", [128, FFNW_COLS], BF16))
        arena = E(nc.sbuf_tensor("arena", [128, HT * S], BF16))
        lpT = E(nc.sbuf_tensor("lpT", [128, 2 * S], BF16))
        zqb = E(nc.sbuf_tensor("zqb", [128, 4 * S], BF16))
        scab = E(nc.sbuf_tensor("scab", [128, S], BF16))
        sceb = E(nc.sbuf_tensor("sceb", [128, S], BF16))
        sctb = E(nc.sbuf_tensor("sctb", [128, S], BF16))
        msel_sb = E(nc.sbuf_tensor("msel_sb", [128, 2 * 512], BF16))
        sqb = E(nc.sbuf_tensor("sqb", [128, 4 * S], BF16))
        tmpf = E(nc.sbuf_tensor("tmpf", [128, 2 * S], F32))
        rep = E(nc.sbuf_tensor("rep", [128, CH], F32))
        rowbuf = E(nc.sbuf_tensor("rowbuf", [1, 1024], F32))
        repb = E(nc.sbuf_tensor("repb", [128, 2 * S], BF16))
        normb = E(nc.sbuf_tensor("normb", [128, 2 * S], BF16))
        rowb16 = E(nc.sbuf_tensor("rowb16", [1, 1024], BF16))
        smb = E(nc.sbuf_tensor("smb", [128, 8 * EXP], F32))
        smeb = E(nc.sbuf_tensor("smeb", [128, 8 * EXP], F32))
        rmeb = E(nc.sbuf_tensor("rmeb", [128, 8], F32))
        rsbb = E(nc.sbuf_tensor("rsbb", [128, 8], F32))
        maskb = E(nc.sbuf_tensor("maskb", [128, 8 * EXP], F32))
        mTb = E(nc.sbuf_tensor("mTb", [EXP, CH], BF16))
        b1t_sb = E(nc.sbuf_tensor("b1t_sb", [128, EXP * HT], F32))
        b2c_sb = E(nc.sbuf_tensor("b2c_sb", [128, EXP * KT], F32))
        wg_sb = E(nc.sbuf_tensor("wg_sb", [128, KT * EXP], BF16))
        wob_sb = E(nc.sbuf_tensor("wob_sb", [128, KT], F32))
        bg_sb = E(nc.sbuf_tensor("bg_sb", [1, EXP], F32))
        bgb = E(nc.sbuf_tensor("bgb", [1, EXP], BF16))
        idf_sb = E(nc.sbuf_tensor("idf_sb", [128, 128], F32))
        onesc_sb = E(nc.sbuf_tensor("onesc_sb", [128, 1], BF16))
        epsc_sb = E(nc.sbuf_tensor("epsc_sb", [1, 1], F32))
        onesr_sb = E(nc.sbuf_tensor("onesr_sb", [1, 128], BF16))
        sel_sb = E(nc.sbuf_tensor("sel_sb", [EXP, EXP * 128], BF16))
        P = [E(nc.psum_tensor(f"p{i}", [128, S], F32)) for i in range(8)]
        sem_T = E(nc.semaphore("semT"))
        sem_V = E(nc.semaphore("semV"))
        sem_S = E(nc.semaphore("semS"))
        sem_DX = E(nc.semaphore("semDX"))
        sem_DFN = E(nc.semaphore("semDFN"))
        sem_DW1 = E(nc.semaphore("semDW1"))
        sem_DW2 = E(nc.semaphore("semDW2"))
        sem_G = E(nc.semaphore("semG"))
        sem_DO0 = E(nc.semaphore("semDO0"))
        sem_DO1 = E(nc.semaphore("semDO1"))
        sem_DR0 = E(nc.semaphore("semDR0"))
        sem_DR1 = E(nc.semaphore("semDR1"))
        block = E(nc.Block())

        sems = {"T": sem_T, "V": sem_V, "S": sem_S, "G": sem_G, "DX": sem_DX,
                "DFN": sem_DFN, "DW1": sem_DW1, "DW2": sem_DW2,
                "DO0": sem_DO0, "DO1": sem_DO1, "DR0": sem_DR0, "DR1": sem_DR1}

        def x_ap(k, j0, n):
            return x_sb[:, k * CH + j0:k * CH + j0 + n]

        def xb_ap(k, j0, n):
            return xbln[:, k * CH + j0:k * CH + j0 + n]

        def w1_ap(k, h):
            return wA[:, k * 2048 + h * 128:k * 2048 + h * 128 + 128]

        def w2_ap(q, dt):
            return wB[:, q * 1024 + dt * 128:q * 1024 + dt * 128 + 128]

        def wqt_ap(k, et):
            return ffnw[:, OF_WQT + k * 1024 + et * 128:OF_WQT + k * 1024 + et * 128 + 128]

        def rtt_ap(et, nt):
            c0 = OF_RTT + et * 256 + nt * 128
            return ffnw[:, c0:c0 + 128]

        def msel_ap(nt, kind, lt):
            c0 = nt * 512 + (0 if kind == "z" else 256) + lt * 128
            return msel_sb[:, c0:c0 + 128]

        def zq_z(nt):
            return zqb[:, nt * S:(nt + 1) * S]

        def zq_q(nt):
            return zqb[:, (2 + nt) * S:(3 + nt) * S]

        def sca(nt):
            return scab[:, 0:S]

        def sct(nt):
            return sctb[:, 0:S]

        def lv_ap(l, dt):
            return ffnw[:, OF_LV + l * 1024 + dt * 128:OF_LV + l * 1024 + dt * 128 + 128]

        def wot_ap(k, dt):
            return ffnw[:, OF_WOT + k * 1024 + dt * 128:OF_WOT + k * 1024 + dt * 128 + 128]

        def hp_ap(h, n=S):
            return arena[:, h * S:h * S + n]

        def ret_ap(k, n=S):
            return arena[:, KT * S + k * S:KT * S + k * S + n]

        def dma3(dst2d, c0, c1, kk, src3d):
            dst = dst2d[:, c0:c1].rearrange("p (k n) -> p k n", k=kk)
            src = src3d.transpose([1, 0, 2])
            return dst, src

        def bankw(i):
            sem, m = B.bank[i]
            return (sem, m)

        for dst, src in ((idf_sb[:], idf), (onesc_sb[:], onesc), (onesr_sb[:], onesr),
                         (sel_sb[:], seli), (wob_sb[:], wob), (bg_sb[:], bg),
                         (epsc_sb[:], epsc)):
            B.emit("sy", lambda sy, d=dst, s=src: sy.dma_start(out=d, in_=s),
                   inc="DX", inc_by=16)
        d, s0 = dma3(b1t_sb, 0, EXP * HT, EXP, b1t)
        B.emit("sy", lambda sy, d=d, s=s0: sy.dma_start(out=d, in_=s), inc="DX", inc_by=16)
        d, s0 = dma3(b2c_sb, 0, EXP * KT, EXP, b2c)
        B.emit("sy", lambda sy, d=d, s=s0: sy.dma_start(out=d, in_=s), inc="DX", inc_by=16)
        m_const = B.mark("DX")
        d = x_sb[:].rearrange("p (k j) -> p k j", k=KT)
        s0 = xT.rearrange("(k p) t -> k p t", p=128)[:, :, 0:CH].transpose([1, 0, 2])
        B.emit("sy", lambda sy, d=d, s=s0: sy.dma_start(out=d, in_=s), inc="DX", inc_by=16)
        m_x = [[B.mark("DX"), B.mark("DX")]] + [None] * nch

        d, s0 = dma3(ffnw, OF_WQT, OF_WQT + 8192, KT, wqt.rearrange("(k p) e -> k p e", p=128))
        B.emit("g", lambda g, d=d, s=s0: g.dma_start(out=d, in_=s), inc="DFN", inc_by=16)
        d, s0 = dma3(ffnw, OF_RTT, OF_RTT + 2048, KT, rtt.rearrange("(k p) n -> k p n", p=128))
        B.emit("g", lambda g, d=d, s=s0: g.dma_start(out=d, in_=s), inc="DFN", inc_by=16)
        d, s0 = dma3(ffnw, OF_LV, OF_LV + 2048, 2, lv.rearrange("(l p) e -> l p e", p=128))
        B.emit("g", lambda g, d=d, s=s0: g.dma_start(out=d, in_=s), inc="DFN", inc_by=16)
        d, s0 = dma3(ffnw, OF_WOT, OF_WOT + 8192, KT, wot.rearrange("(k p) e -> k p e", p=128))
        B.emit("g", lambda g, d=d, s=s0: g.dma_start(out=d, in_=s), inc="DFN", inc_by=16)
        d, s0 = dma3(wg_sb, 0, KT * EXP, KT, wg)
        B.emit("g", lambda g, d=d, s=s0: g.dma_start(out=d, in_=s), inc="DFN", inc_by=16)
        d, s0 = dma3(msel_sb, 0, 2 * 512, 2, msel.rearrange("(nt p) e -> nt p e", p=128))
        B.emit("g", lambda g, d=d, s=s0: g.dma_start(out=d, in_=s), inc="DFN", inc_by=16)
        m_ffnw = B.mark("DFN")

        B.emit("s", lambda s: s.mul(ffnw[:, OF_LV:OF_LV + 2048], ffnw[:, OF_LV:OF_LV + 2048],
                                    0.5), waits=[("DFN", m_ffnw)], inc="S")
        m_lvscale = B.mark("S")
        B.emit("v", lambda v: v.tensor_scalar_mul(wob_sb[:], wob_sb[:], 0.5),
               waits=[("DX", m_x[0][0])], inc="V")
        B.emit("v", lambda v: v.tensor_copy(bgb[:], bg_sb[:]), waits=[("DX", m_x[0][0])], inc="V")
        m_pro_v = B.mark("V")

        state = {"wA": None, "wB": None}
        mw1 = {}
        mw2 = {}

        def load_w1(e, wait_T):
            if state["wA"] == e:
                return
            d = wA[:].rearrange("p (k h) -> p k h", k=KT)
            s0 = w1[e].rearrange("(k p) h -> k p h", p=128).transpose([1, 0, 2])
            B.emit("g", lambda g, d=d, s=s0: g.dma_start(out=d, in_=s),
                   waits=[("T", wait_T)], inc="DW1", inc_by=16)
            state["wA"] = e
            mw1[e] = B.mark("DW1")

        def load_w2(e, wait_T):
            if state["wB"] == e:
                return
            d = wB[:].rearrange("p (q dd) -> p q dd", q=HT)
            s0 = w2[e].rearrange("(q p) dd -> q p dd", p=128).transpose([1, 0, 2])
            B.emit("g", lambda g, d=d, s=s0: g.dma_start(out=d, in_=s),
                   waits=[("T", wait_T)], inc="DW2", inc_by=16)
            state["wB"] = e
            mw2[e] = B.mark("DW2")

        load_w1(0, 0)
        load_w2(0, 0)

        marks = {
            "xbln_free_T": 0,
            "arena_free_T": 0,
            "rep_free_V": 0,
        }
        xmark = {(k, s): 0 for k in range(KT) for s in range(2)}
        tmark = {0: 0, 1: 0}
        castmark = {(k, s): ("S", 0) for k in range(KT) for s in range(2)}
        ln_sqfree = {j: 0 for j in range(8)}
        ln_rowfree = {"m": []}
        ln_rb16free = {"m": 0}
        ln_repbfree = {"m": 0}
        ln_normbfree = {"m": 0}
        ffn_state = {"zq_free_T": {i: 0 for i in range(4)},
                     "sca_free_V": {0: 0}, "sct_free_V": {0: 0},
                     "sce_free_S": 0, "lpT_free_T": 0}

        def make_LN(c, dep, cast_in=False):
            mxln = [None, None]
            msub_ks = [None, None]

            def rb(c0, c1):
                return rowbuf[0:1, c0:c1]

            def half(s):
                b0, b1 = (4, 5) if s == 0 else (6, 7)
                if cast_in:
                    for k in range(KT):
                        w = [("T", marks["xbln_free_T"]), dep(k, s),
                             ("V", xmark[(k, s)])]
                        B.emit("s", lambda ss, k=k, s=s: ss.copy(
                            xb_ap(k, s * S, S), x_ap(k, s * S, S)), waits=w, inc="S")
                        castmark[(k, s)] = ("S", B.mark("S"))
                for k in range(KT):
                    w = [castmark[(k, s)]]
                    if k == 0:
                        w.append(bankw(b0))
                    B.emit("t", lambda t, s=s, k=k, b0=b0: t.matmul(
                        P[b0][0:1, :], onesc_sb[:], xb_ap(k, s * S, S),
                        start=(k == 0), stop=(k == KT - 1)),
                        waits=w, inc="T" if k == KT - 1 else None)
                msum = B.mark("T")
                for k in range(KT):
                    j = k % 4
                    gw = [castmark[(k, s)], ("T", ln_sqfree[j])]
                    B.emit("v", lambda v, k=k, s=s, j=j: v.tensor_mul(
                        sqb[:, j * S:(j + 1) * S], xb_ap(k, s * S, S),
                        xb_ap(k, s * S, S)), waits=gw, inc="V")
                    msqk = B.mark("V")
                    w = [("V", msqk)]
                    if k == 0:
                        w.append(bankw(b1))
                    B.emit("t", lambda t, k=k, j=j, b1=b1: t.matmul(
                        P[b1][0:1, :], onesc_sb[:], sqb[:, j * S:(j + 1) * S],
                        start=(k == 0), stop=(k == KT - 1)),
                        waits=w, inc="T")
                    ln_sqfree[j] = B.mark("T")
                msumsq = B.mark("T")
                B.emit("s", lambda ss, b0=b0: ss.activation(
                    rb(0, 512), P[b0][0:1, :], AF.Square, scale=1.0 / D),
                    waits=[("T", msum)] + ln_rowfree["m"], inc="S")
                mm2 = B.mark("S")
                B.emit("v", lambda v, b1=b1: v.scalar_tensor_tensor(
                    rb(512, 1024), P[b1][0:1, :], 1.0 / D, rb(0, 512),
                    ALU.mult, ALU.subtract),
                    waits=[("T", msumsq), ("S", mm2)] + ln_rowfree["m"], inc="V")
                mvar = B.mark("V")
                B.bank[b1] = ("V", mvar)
                B.emit("s", lambda ss, b0=b0: ss.mul(
                    rowb16[0:1, 0:512], P[b0][0:1, :], 1.0 / D),
                    waits=[("T", msum), ("T", ln_rb16free["m"])], inc="S")
                mmean = B.mark("S")
                B.bank[b0] = ("S", mmean)
                B.emit("s", lambda ss: ss.activation(
                    rb(0, 512), rb(512, 1024), AF.Ln, bias=epsc_sb[0:1, 0:1]),
                    waits=[("V", mvar)], inc="S")
                mlnv = B.mark("S")
                B.emit("s", lambda ss: ss.activation(
                    rowb16[0:1, 512:1024], rb(0, 512), AF.Exp, scale=-0.5),
                    waits=[("S", mlnv), ("T", ln_rb16free["m"])], inc="S")
                mrstd = B.mark("S")
                ln_rowfree["m"] = [("S", mrstd), ("V", mvar)]
                B.emit("t", lambda t, b0=b0: t.matmul(
                    P[b0][:, :], onesr_sb[:], rowb16[0:1, 512:1024],
                    start=True, stop=True), waits=[("S", mrstd), bankw(b0)])
                B.emit("t", lambda t, b1=b1: t.matmul(
                    P[b1][:, :], onesr_sb[:], rowb16[0:1, 0:512],
                    start=True, stop=True), waits=[("S", mmean), bankw(b1)], inc="T")
                mrep = B.mark("T")
                ln_rb16free["m"] = mrep
                B.emit("s", lambda ss, b1=b1: ss.copy(repb[:, 512:1024], P[b1][:, :]),
                       waits=[("T", mrep), ("V", ln_repbfree["m"])], inc="S")
                mevm = B.mark("S")
                B.bank[b1] = ("S", mevm)
                B.emit("s", lambda ss, b0=b0: ss.copy(repb[:, 0:512], P[b0][:, :]),
                       waits=[("T", mrep), ("V", ln_repbfree["m"])], inc="S")
                mevr = B.mark("S")
                B.bank[b0] = ("S", mevr)
                seq = []
                for k in range(KT):
                    seq.append(("sub", k))
                    if k >= 1:
                        seq.append(("mul", k - 1))
                seq.append(("mul", KT - 1))
                mmulk = {}
                msubk = {}
                for kind, k in seq:
                    if kind == "sub":
                        w = [("S", mevm), castmark[(k, s)]]
                        if k >= 2:
                            w.append(("V", mmulk[k - 2]))
                        else:
                            w.append(("V", ln_normbfree["m"]))
                        B.emit("v", lambda v, k=k, s=s: v.tensor_tensor(
                            normb[:, (k % 2) * S:(k % 2 + 1) * S], xb_ap(k, s * S, S),
                            repb[:, 512:1024], ALU.subtract), waits=w, inc="V")
                        msubk[k] = B.mark("V")
                    else:
                        B.emit("v", lambda v, k=k, s=s: v.tensor_tensor(
                            xb_ap(k, s * S, S), normb[:, (k % 2) * S:(k % 2 + 1) * S],
                            repb[:, 0:512], ALU.mult),
                            waits=[("V", msubk[k]), ("S", mevr)], inc="V")
                        mmulk[k] = B.mark("V")
                mxln[s] = B.mark("V")
                msub_ks[s] = [mmulk[k] for k in range(KT)]
                ln_repbfree["m"] = mxln[s]
                ln_normbfree["m"] = mxln[s]

            return (lambda: half(0)), (lambda: half(1)), mxln, msub_ks

        def make_FFN(c, cast=False, store=False):
            QB = {0: (0, 1), 1: (2, 3)}
            st = ffn_state
            H = {0: {"mfe": {}}, 1: {"mfe": {}}}
            MS = {}
            done = {"q0": False}

            def q_group(s, mids=None):
                qb = QB[s]
                mq = []
                mqe = []
                for et in range(EXP):
                    if mids and et in mids:
                        mids[et]()
                    for k in range(KT):
                        w = [("V", MS[s][k]), ("DFN", m_ffnw)]
                        if k == 0:
                            if et >= 2:
                                w.append(("S", mqe[et - 2]))
                            else:
                                w.append(bankw(qb[et % 2]))
                        B.emit("t", lambda t, et=et, k=k, s=s, qb=qb: t.matmul(
                            P[qb[et % 2]][:, :], wqt_ap(k, et), xb_ap(k, s * S, S),
                            start=(k == 0), stop=(k == KT - 1)),
                            waits=w, inc="T" if k == KT - 1 else None)
                    mq.append(B.mark("T"))
                    ew = [("T", mq[et]), ("T", marks["arena_free_T"])]
                    if s == 1:
                        ew.append(("T", H[0]["m_z_end"]))
                    B.emit("s", lambda ss, et=et, qb=qb: ss.copy(
                        hp_ap(et), P[qb[et % 2]][:, :]), waits=ew, inc="S")
                    mqe.append(B.mark("S"))
                B.bank[qb[0]] = ("S", mqe[6])
                B.bank[qb[1]] = ("S", mqe[7])
                H[s]["mqe"] = mqe

            def z_group(s):
                mz = []
                for nt in range(2):
                    for et in range(EXP):
                        w = [("S", H[s]["mqe"][et])]
                        if et == 0:
                            w.append(bankw(4 + nt))
                        B.emit("t", lambda t, nt=nt, et=et: t.matmul(
                            P[4 + nt][:, :], rtt_ap(et, nt), hp_ap(et),
                            start=(et == 0), stop=(et == EXP - 1)),
                            waits=w, inc="T" if et == EXP - 1 else None)
                    mz.append(B.mark("T"))
                H[s]["mz"] = mz
                H[s]["m_z_end"] = B.mark("T")

            def schain_group(s):
                mcp = {}
                mq2 = {}
                for nt in range(2):
                    B.emit("s", lambda ss, nt=nt: ss.copy(zq_z(nt), P[4 + nt][:, :]),
                           waits=[("T", H[s]["mz"][nt]),
                                  ("T", st["zq_free_T"][nt])], inc="S")
                    mcp[nt] = B.mark("S")
                    B.emit("s", lambda ss, nt=nt: ss.activation(
                        sca(nt), P[4 + nt][:, :], AF.Abs),
                        waits=[("S", mcp[nt]), ("V", st["sca_free_V"][0])], inc="S")
                    mab = B.mark("S")
                    B.bank[4 + nt] = ("S", mab)
                    B.emit("s", lambda ss, nt=nt: ss.activation(
                        sceb[:, :], sca(nt), AF.Exp, scale=-1.0),
                        waits=[("S", mab), ("S", st["sce_free_S"])], inc="S")
                    mex = B.mark("S")
                    B.emit("s", lambda ss, nt=nt: ss.activation(
                        sct(nt), sceb[:, :], AF.Ln, bias=1.0),
                        waits=[("S", mex), ("V", st["sct_free_V"][0])], inc="S")
                    mln = B.mark("S")
                    st["sce_free_S"] = mln
                    B.emit("v", lambda v, nt=nt: v.scalar_tensor_tensor(
                        zq_q(nt), sca(nt), 0.5, sct(nt), ALU.mult, ALU.add),
                        waits=[("S", mln), ("T", st["zq_free_T"][2 + nt])], inc="V")
                    mq2[nt] = B.mark("V")
                    st["sca_free_V"][0] = mq2[nt]
                    st["sct_free_V"][0] = mq2[nt]
                H[s]["mcp"] = mcp
                H[s]["mq2"] = mq2

            def ll_group(s):
                mll = []
                for lt in range(2):
                    ops = [("z", 0), ("z", 1), ("q", 0), ("q", 1)]
                    for i, (kind, nt) in enumerate(ops):
                        dep = (("S", H[s]["mcp"][nt]) if kind == "z"
                               else ("V", H[s]["mq2"][nt]))
                        w = [dep, ("DFN", m_ffnw)]
                        if i == 0:
                            w.append(bankw(6 + lt))
                        B.emit("t", lambda t, kind=kind, nt=nt, lt=lt, i=i: t.matmul(
                            P[6 + lt][:, :], msel_ap(nt, kind, lt),
                            (zq_z(nt) if kind == "z" else zq_q(nt)),
                            start=(i == 0), stop=(i == 3)),
                            waits=w, inc="T" if i == 3 else None)
                    mll.append(B.mark("T"))
                mend = B.mark("T")
                for i in range(4):
                    st["zq_free_T"][i] = mend
                mexp = []
                for lt in range(2):
                    B.emit("s", lambda ss, lt=lt: ss.activation(
                        lpT[:, lt * S:(lt + 1) * S], P[6 + lt][:, :], AF.Exp),
                        waits=[("T", mll[lt]), ("T", st["lpT_free_T"])], inc="S")
                    mexp.append(B.mark("S"))
                    B.bank[6 + lt] = ("S", mexp[lt])
                H[s]["mexp"] = mexp

            def ret_group(s):
                qb = QB[s]
                slot = ret_ap if s == 0 else hp_ap
                mret = []
                mrete = []
                for dt in range(KT):
                    for l in range(2):
                        w = [("S", H[s]["mexp"][l]), ("S", m_lvscale)]
                        if l == 0:
                            if dt >= 2:
                                w.append(("S", mrete[dt - 2]))
                            else:
                                w.append(bankw(qb[dt % 2]))
                        B.emit("t", lambda t, dt=dt, l=l, qb=qb: t.matmul(
                            P[qb[dt % 2]][:, :], lv_ap(l, dt), lpT[:, l * S:(l + 1) * S],
                            start=(l == 0), stop=(l == 1)),
                            waits=w, inc="T" if l == 1 else None)
                    mret.append(B.mark("T"))
                    ew = [("T", mret[dt])]
                    if s == 0:
                        ew.append(("T", marks["arena_free_T"]))
                    else:
                        ew.append(("T", H[1]["m_z_end"]))
                    B.emit("s", lambda ss, dt=dt, qb=qb, slot=slot: ss.copy(
                        slot(dt), P[qb[dt % 2]][:, :]), waits=ew, inc="S")
                    mrete.append(B.mark("S"))
                B.bank[qb[0]] = ("S", mrete[6])
                B.bank[qb[1]] = ("S", mrete[7])
                H[s]["mrete"] = mrete
                H[s]["slot"] = slot
                st["lpT_free_T"] = B.mark("T")

            def f_group(s, dts):
                qb = QB[s]
                slot = H[s]["slot"]
                mfe = H[s]["mfe"]
                for dt in dts:
                    for k in range(KT):
                        w = [("S", H[s]["mrete"][k])]
                        if k == 0:
                            if dt >= 2:
                                w.append(("V", mfe[dt - 2]))
                            else:
                                w.append(bankw(qb[dt % 2]))
                        B.emit("t", lambda t, dt=dt, k=k, qb=qb, slot=slot: t.matmul(
                            P[qb[dt % 2]][:, :], wot_ap(k, dt), slot(k),
                            start=(k == 0), stop=(k == KT - 1)),
                            waits=w, inc="T" if k == KT - 1 else None)
                    mfdt = B.mark("T")
                    B.emit("v", lambda v, dt=dt, s=s, qb=qb: v.scalar_tensor_tensor(
                        x_ap(dt, s * S, S), P[qb[dt % 2]][:, :], wob_sb[:, dt:dt + 1],
                        x_ap(dt, s * S, S), ALU.add, ALU.add),
                        waits=[("T", mfdt), ("V", m_pro_v), ("V", xmark[(dt, s)])],
                        inc="V")
                    mfe[dt] = B.mark("V")
                    xmark[(dt, s)] = B.mark("V")
                    if cast:
                        B.emit("s", lambda ss, dt=dt, s=s: ss.copy(
                            xb_ap(dt, s * S, S), x_ap(dt, s * S, S)),
                            waits=[("V", xmark[(dt, s)])], inc="S")
                        castmark[(dt, s)] = ("S", B.mark("S"))
                    if store:
                        dst = out[dt * 128:(dt + 1) * 128,
                                  c * CH + s * S:c * CH + s * S + S]
                        src = x_ap(dt, s * S, S)
                        B.emit("sy", lambda sy, dst=dst, src=src: sy.dma_start(
                            out=dst, in_=src), waits=[("V", xmark[(dt, s)])],
                            inc="DO%d" % s, inc_by=16)
                if store and dts[-1] == KT - 1:
                    mo = B.mark("DO%d" % s)
                    if c + 1 < nch:
                        for k in range(KT):
                            s0_ = xT[k * 128:(k + 1) * 128,
                                     (c + 1) * CH + s * S:(c + 1) * CH + s * S + S]
                            B.emit("sy", lambda sy, k=k, s=s, s0_=s0_: sy.dma_start(
                                out=x_ap(k, s * S, S), in_=s0_),
                                waits=[("DO%d" % s, mo)], inc="DR%d" % s, inc_by=16)
                        m_x[c + 1][s] = B.mark("DR%d" % s)
                if dts[-1] == KT - 1:
                    B.bank[qb[0]] = ("V", mfe[6])
                    B.bank[qb[1]] = ("V", mfe[7])

            if store and c + 1 < nch:
                m_x[c + 1] = [None, None]

            def q0(msk0):
                MS[0] = msk0
                q_group(0)
                done["q0"] = True

            def rest(msk, f1_hook=None):
                MS[0], MS[1] = msk[0], msk[1]
                if not done["q0"]:
                    q_group(0)
                z_group(0)
                schain_group(0)
                q_group(1, mids={2: lambda: ll_group(0)})
                z_group(1)
                schain_group(1)
                ret_group(0)
                f_group(0, [0, 1, 2, 3])
                ll_group(1)
                f_group(0, [4, 5, 6, 7])
                ret_group(1)
                f_group(1, [0, 1, 2, 3])
                if f1_hook is not None:
                    f1_hook()
                f_group(1, [4, 5, 6, 7])
                marks["arena_free_T"] = B.mark("T")
                marks["xbln_free_T"] = B.mark("T")
            return q0, rest

        def make_MASK(c, mstop="full"):
            mlg = []

            def gate_groups(cb8s, msk_s):
                for cb8 in cb8s:
                    for k in range(KT):
                        w = [("V", msk_s[k]), ("DFN", m_ffnw)]
                        if k == 0 and cb8 == 0:
                            w.append(bankw(0))
                        B.emit("t", lambda t, cb8=cb8, k=k: t.matmul(
                            P[0][:, cb8 * EXP:(cb8 + 1) * EXP],
                            xbln[:, k * CH + cb8 * 128:k * CH + cb8 * 128 + 128],
                            wg_sb[:, k * EXP:(k + 1) * EXP], start=(k == 0), stop=False),
                            waits=w)
                    B.emit("t", lambda t, cb8=cb8: t.matmul(
                        P[0][:, cb8 * EXP:(cb8 + 1) * EXP], onesr_sb[:], bgb[:],
                        start=False, stop=True), waits=[("V", m_pro_v)], inc="T")
                    mlg.append(B.mark("T"))

            def gate0(msk0):
                gate_groups(range(4), msk0)

            def rest(msk):
                if not mlg:
                    gate_groups(range(4), msk[0])
                gate_groups(range(4, 8), msk[1])
                return softmax_part()

            def softmax_part():
                B.emit("s", lambda ss: ss.copy(smb[:, 0:8 * EXP], P[0][:, 0:8 * EXP]),
                       waits=[("T", mlg[7])], inc="S")
                mlge = B.mark("S")
                B.bank[0] = ("S", mlge)
                if mstop == "lg":
                    return None
                def g3(buf):
                    return buf[:, 0:8 * EXP].rearrange("p (g e) -> p g e", g=8)

                def r3(buf):
                    return buf[:, 0:8].unsqueeze(2)

                def r3b(buf):
                    return buf[:, 0:8].unsqueeze(2).broadcast_to([128, 8, EXP])
                B.emit("v", lambda v: v.reduce_max(r3(rmeb), g3(smb), AX.X),
                       waits=[("S", mlge)], inc="V")
                mrm = B.mark("V")
                B.emit("v", lambda v: v.tensor_tensor(
                    g3(smeb), g3(smb), r3b(rmeb), ALU.subtract),
                    waits=[("V", mrm)], inc="V")
                msub = B.mark("V")
                if mstop == "sub":
                    return None
                B.emit("s", lambda ss: ss.activation(smb[:], smeb[:], AF.Exp),
                       waits=[("V", msub)], inc="S")
                mexp = B.mark("S")
                if mstop == "exp":
                    return None
                B.emit("v", lambda v: v.reduce_sum(r3(rsbb), g3(smb), AX.X),
                       waits=[("S", mexp)], inc="V")
                mrs = B.mark("V")
                B.emit("v", lambda v: v.reciprocal(rsbb[:, 0:8], rsbb[:, 0:8]),
                       waits=[("V", mrs)], inc="V")
                mrcp = B.mark("V")
                if mstop == "recip":
                    return None
                B.emit("v", lambda v: v.tensor_tensor(
                    g3(smeb), g3(smb), r3b(rsbb), ALU.mult),
                    waits=[("V", mrcp)], inc="V")
                mgw = B.mark("V")
                B.emit("v", lambda v: v.reduce_max(r3(rmeb), g3(smeb), AX.X),
                       waits=[("V", mgw)], inc="V")
                mrm2 = B.mark("V")
                B.emit("v", lambda v: v.tensor_tensor(
                    g3(maskb), g3(smeb), r3b(rmeb), ALU.is_ge),
                    waits=[("V", mrm2)], inc="V")
                mge = B.mark("V")
                B.emit("v", lambda v: v.tensor_mul(maskb[:], maskb[:], smeb[:]),
                       waits=[("V", mge)], inc="V")
                mmask = B.mark("V")
                return mmask
            return gate0, rest

        def emit_MASK_finish(mmask):
            for cb8 in range(8):
                bank = 4 if cb8 < 4 else 5
                cc = (cb8 % 4) * 128
                w = [("V", mmask)]
                if cb8 % 4 == 0:
                    w.append(bankw(bank))
                B.emit("t", lambda t, cb8=cb8, bank=bank, cc=cc: t.transpose(
                    P[bank][0:EXP, cc:cc + 128], maskb[:, cb8 * EXP:(cb8 + 1) * EXP], idf_sb[:]),
                    waits=w, inc="T" if cb8 == 7 else None)
            mmt = B.mark("T")
            B.emit("v", lambda v: v.tensor_copy(mTb[0:EXP, 0:S], P[4][0:EXP, :]),
                   waits=[("T", mmt)], inc="V")
            B.emit("v", lambda v: v.tensor_copy(mTb[0:EXP, S:CH], P[5][0:EXP, :]),
                   inc="V")
            mmaskT = B.mark("V")
            B.bank[4] = ("V", mmaskT)
            B.bank[5] = ("V", mmaskT)
            return mmaskT

        def make_ATTN(c):
            order = list(range(EXP)) if c % 2 == 0 else list(range(EXP - 1, -1, -1))
            ST = {"mmaskT": None, "mrepe": None}
            MS = {}

            def emit_mm1(e, s):
                mm1 = []
                mhe = []
                for h in range(HT):
                    for k in range(KT):
                        w = [("DW1", mw1[e]), ("V", MS[s][k])]
                        if k == 0:
                            if h >= 2:
                                w.append(("S", mhe[h - 2]))
                            else:
                                w.append(bankw(h % 2))
                        B.emit("t", lambda t, h=h, k=k, s=s: t.matmul(
                            P[h % 2][:, :], w1_ap(k, h), xb_ap(k, s * S, S),
                            start=(k == 0), stop=(k == KT - 1)),
                            waits=w, inc="T" if k == KT - 1 else None)
                    mm1.append(B.mark("T"))
                    B.emit("s", lambda ss, h=h, e=e: ss.activation(
                        hp_ap(h), P[h % 2][:, :], AF.Relu,
                        bias=b1t_sb[:, e * HT + h:e * HT + h + 1]),
                        waits=[("T", mm1[h]), ("T", marks["arena_free_T"])], inc="S")
                    mhe.append(B.mark("S"))
                B.bank[0] = ("S", mhe[14])
                B.bank[1] = ("S", mhe[15])
                return mhe, B.mark("T")

            def emit_rep(e):
                B.emit("t", lambda t, e=e: t.matmul(
                    P[4][:, :], sel_sb[:, e * 128:(e + 1) * 128], mTb[0:EXP, 0:S],
                    start=True, stop=True),
                    waits=[("V", ST["mmaskT"]), ("V", marks["rep_free_V"]), bankw(4)])
                B.emit("t", lambda t, e=e: t.matmul(
                    P[5][:, :], sel_sb[:, e * 128:(e + 1) * 128], mTb[0:EXP, S:CH],
                    start=True, stop=True), waits=[bankw(5)], inc="T")
                mrepm = B.mark("T")
                B.emit("v", lambda v: v.tensor_copy(rep[:, 0:S], P[4][:, :]),
                       waits=[("T", mrepm), ("V", marks["rep_free_V"])], inc="V")
                B.emit("v", lambda v: v.tensor_copy(rep[:, S:CH], P[5][:, :]), inc="V")
                mrepe = B.mark("V")
                B.bank[4] = ("V", mrepe)
                B.bank[5] = ("V", mrepe)
                ST["mrepe"] = mrepe
                return mrepe

            def emit_mm2(e, s, mhe, midfns, last):
                banks = [2, 3, 6, 7] if midfns else [2, 3]
                nb = len(banks)
                mm2 = {}
                matt = {}
                pend = [None]
                pending = []

                def flush_one(dt):
                    bk = banks[dt % nb]
                    w = [("T", mm2[dt]), ("V", ST["mrepe"]), ("V", tmark[dt % 2])]
                    B.emit("v", lambda v, dt=dt, e=e, s=s, bk=bk: v.scalar_tensor_tensor(
                        tmpf[:, (dt % 2) * S:(dt % 2 + 1) * S], P[bk][:, :],
                        b2c_sb[:, e * KT + dt:e * KT + dt + 1], rep[:, s * S:(s + 1) * S],
                        ALU.add, ALU.mult),
                        waits=w, inc="V")
                    matt[dt] = B.mark("V")
                    if pend[0] is not None:
                        pdt, pfn = pend[0]
                        B.emit("v", pfn, waits=[("V", matt[pdt]),
                                                ("V", xmark[(pdt, s)])], inc="V")
                        xmark[(pdt, s)] = B.mark("V")
                        tmark[pdt % 2] = B.mark("V")
                        if last:
                            B.emit("s", lambda ss, pdt=pdt, s=s: ss.copy(
                                xb_ap(pdt, s * S, S), x_ap(pdt, s * S, S)),
                                waits=[("V", xmark[(pdt, s)])], inc="S")
                            castmark[(pdt, s)] = ("S", B.mark("S"))
                    pend[0] = (dt, (lambda v, dt=dt, s=s: v.tensor_add(
                        x_ap(dt, s * S, S), x_ap(dt, s * S, S),
                        tmpf[:, (dt % 2) * S:(dt % 2 + 1) * S])))

                for dt in range(KT):
                    if midfns and dt in midfns:
                        midfns[dt]()
                    if ST["mrepe"] is not None:
                        while pending:
                            flush_one(pending.pop(0))
                    bk = banks[dt % nb]
                    for q in range(HT):
                        w = [("DW2", mw2[e]), ("S", mhe[q])]
                        if q == 0:
                            if dt >= nb:
                                w.append(("V", matt[dt - nb]))
                            else:
                                w.append(bankw(bk))
                        B.emit("t", lambda t, dt=dt, q=q, bk=bk: t.matmul(
                            P[bk][:, :], w2_ap(q, dt), hp_ap(q),
                            start=(q == 0), stop=(q == HT - 1)),
                            waits=w, inc="T" if q == HT - 1 else None)
                    mm2[dt] = B.mark("T")
                    pending.append(dt)
                    if ST["mrepe"] is not None:
                        while pending:
                            flush_one(pending.pop(0))
                while pending:
                    flush_one(pending.pop(0))
                pdt, pfn = pend[0]
                B.emit("v", pfn, waits=[("V", matt[pdt]), ("V", xmark[(pdt, s)])], inc="V")
                xmark[(pdt, s)] = B.mark("V")
                tmark[pdt % 2] = B.mark("V")
                if last:
                    B.emit("s", lambda ss, pdt=pdt, s=s: ss.copy(
                        xb_ap(pdt, s * S, S), x_ap(pdt, s * S, S)),
                        waits=[("V", xmark[(pdt, s)])], inc="S")
                    castmark[(pdt, s)] = ("S", B.mark("S"))
                for i, b in enumerate(banks):
                    B.bank[b] = ("V", matt[KT - nb + i])

            def mm1_first(msk0):
                MS[0] = msk0
                ST["first"] = emit_mm1(order[0], 0)

            def rest(msk, mask_fn, ln3_half0=None):
                MS[0], MS[1] = msk[0], msk[1]
                for ei, e in enumerate(order):
                    last = (ei == EXP - 1)
                    for s in range(2):
                        if ei == 0 and s == 0:
                            mhe, m_mm1_end = ST["first"]
                            mk = {"mmask": None}

                            def mid_gates():
                                mk["mmask"] = mask_fn()

                            def mid_finish():
                                ST["mmaskT"] = emit_MASK_finish(mk["mmask"])
                                emit_rep(e)
                            ST["mrepe"] = None
                            emit_mm2(e, s, mhe, {2: mid_gates, 4: mid_finish}, last)
                        else:
                            mhe, m_mm1_end = emit_mm1(e, s)
                            if s == 0:
                                emit_rep(e)
                            if s == 1 and last and ln3_half0 is not None:
                                ln3_half0()
                            if s == 1 and ei < EXP - 1:
                                load_w1(order[ei + 1], m_mm1_end)
                            emit_mm2(e, s, mhe, None, last)
                        marks["arena_free_T"] = B.mark("T")
                        if s == 1 and ei < EXP - 1:
                            load_w2(order[ei + 1], marks["arena_free_T"])
                    marks["rep_free_V"] = B.mark("V")
                marks["xbln_free_T"] = B.mark("T")
            return mm1_first, rest

        def done(phase):
            order = ["ln1", "ffn1", "ln2", "mask", "attn", "ln3", "ffn2", "full"]
            return order.index(stop_after) < order.index(phase)

        def lnd(cc):
            def dep(k, s, cc=cc):
                if cc == 0:
                    return ("DX", m_x[cc][s])
                return ("DR%d" % s, m_x[cc][s])
            return dep

        ln1_h0, ln1_h1, ln1_mxln, ln1_msk = make_LN(0, lnd(0), cast_in=True)
        ln1_h0()
        for c in range(nch):
            ffn1_q0, ffn1_rest = make_FFN(c, cast=True)
            ffn1_q0(ln1_msk[0])
            ln1_h1()
            if done("ffn1"):
                break
            ln2_h0, ln2_h1, ln2_mxln, ln2_msk = make_LN(c, None)
            ffn1_rest(ln1_msk, f1_hook=ln2_h0)
            if done("ln2"):
                break
            attn_mm1, attn_rest = make_ATTN(c)
            attn_mm1(ln2_msk[0])
            ln2_h1()
            if done("mask"):
                break
            _, mask_rest = make_MASK(c, mstop=mask_stop)
            ln3_h0, ln3_h1, ln3_mxln, ln3_msk = make_LN(c, None)
            attn_rest(ln2_msk, mask_fn=lambda: mask_rest(ln2_msk),
                      ln3_half0=ln3_h0)
            if done("attn") or done("ln3"):
                break
            ffn2_q0, ffn2_rest = make_FFN(c, cast=False, store=True)
            ffn2_q0(ln3_msk[0])
            ln3_h1()
            if done("ffn2"):
                break
            if c + 1 < nch:
                ln1_h0, ln1_h1, ln1_mxln, ln1_msk = make_LN(
                    c + 1, lnd(c + 1), cast_in=True)
                ffn2_rest(ln3_msk, f1_hook=ln1_h0)
            else:
                ffn2_rest(ln3_msk)

        if stop_after != "full":
            wd = [("V", B.mark("V")), ("S", B.mark("S")), ("T", B.mark("T"))]
            dst = out.rearrange("(k p) t -> k p t", p=128)[:, :, 0:CH].transpose([1, 0, 2])
            src = x_sb[:].rearrange("p (k j) -> p k j", k=KT)
            B.emit("sy", lambda sy, dst=dst, src=src: sy.dma_start(out=dst, in_=src),
                   waits=wd, inc="DX", inc_by=16)
            dst = out.rearrange("(k p) t -> k p t", p=128)[:, :, CH:2 * CH].transpose([1, 0, 2])
            src = xbln[:].rearrange("p (k j) -> p k j", k=KT)
            B.emit("g", lambda g, dst=dst, src=src: g.dma_start(out=dst, in_=src),
                   waits=wd, inc="DX", inc_by=16)

        m_final = B.mark("DX")

        m_do0 = B.mark("DO0")
        m_do1 = B.mark("DO1")

        @block.sync
        def _(sy):
            B.play("sy", sy, sems)
            sy.wait_ge(sem_DX, m_final)
            if m_do0:
                sy.wait_ge(sem_DO0, m_do0)
            if m_do1:
                sy.wait_ge(sem_DO1, m_do1)

        @block.gpsimd
        def _(g):
            B.play("g", g, sems)

        @block.tensor
        def _(t):
            B.play("t", t, sems)

        @block.vector
        def _(v):
            B.play("v", v, sems)

        @block.scalar
        def _(s):
            B.play("s", s, sems)

    return nc


def _host_inputs(L0_attn_Wg, L0_attn_bg, L0_attn_W1, L0_attn_b1, L0_attn_W2, L0_attn_b2,
                 L0_ffn_Wq, L0_ffn_routers, L0_ffn_leaf_values, L0_ffn_Wout_w, L0_ffn_Wout_b):
    f = np.float32
    shared = {
        "w1": np.ascontiguousarray(L0_attn_W1, dtype=f),
        "w2": np.ascontiguousarray(L0_attn_W2, dtype=f),
        "b1t": np.ascontiguousarray(np.asarray(L0_attn_b1, dtype=f).reshape(EXP, HT, 128)
                                    .transpose(0, 2, 1)),
        "b2c": np.ascontiguousarray(np.asarray(L0_attn_b2, dtype=f).reshape(EXP, KT, 128)
                                    .transpose(0, 2, 1)),
        "wg": np.ascontiguousarray(np.asarray(L0_attn_Wg, dtype=f).reshape(KT, 128, EXP)),
        "bg": np.ascontiguousarray(np.asarray(L0_attn_bg, dtype=f).reshape(1, EXP)),
        "wqt": np.ascontiguousarray(np.asarray(L0_ffn_Wq, dtype=f).T),
        "rtt": np.ascontiguousarray(np.concatenate(
            [np.asarray(L0_ffn_routers, dtype=f).T, np.zeros((D, 1), dtype=f)], axis=1)),
        "lv": np.ascontiguousarray(L0_ffn_leaf_values, dtype=f),
        "wot": np.ascontiguousarray(np.asarray(L0_ffn_Wout_w, dtype=f).T),
        "wob": np.ascontiguousarray(np.asarray(L0_ffn_Wout_b, dtype=f).reshape(KT, 128).T),
        "idf": np.eye(128, dtype=f),
        "onesc": np.ones((128, 1), dtype=ml_dtypes.bfloat16),
        "epsc": np.full((1, 1), EPS, dtype=f),
        "onesr": np.ones((1, 128), dtype=ml_dtypes.bfloat16),
    }
    sel = np.zeros((EXP, EXP * 128), dtype=ml_dtypes.bfloat16)
    for e in range(EXP):
        sel[e, e * 128:(e + 1) * 128] = 1
    shared["seli"] = sel
    ms = np.zeros((256, 512), dtype=np.float32)
    for m in range(256):
        for lev in range(8):
            j = (1 << lev) - 1 + (m >> (8 - lev))
            right = (m >> (7 - lev)) & 1
            ms[j, m] = 0.5 if right else -0.5
            ms[j, 256 + m] = -1.0
    shared["msel"] = ms.astype(ml_dtypes.bfloat16)
    return shared


def kernel(L0_attn_x, L0_attn_Wg, L0_attn_bg, L0_attn_W1, L0_attn_b1, L0_attn_W2, L0_attn_b2,
           L0_ffn_Wq, L0_ffn_routers, L0_ffn_leaf_keys, L0_ffn_leaf_values,
           L0_ffn_Wout_w, L0_ffn_Wout_b):
    x = np.asarray(L0_attn_x, dtype=np.float32)
    shared = _host_inputs(L0_attn_Wg, L0_attn_bg, L0_attn_W1, L0_attn_b1, L0_attn_W2,
                          L0_attn_b2, L0_ffn_Wq, L0_ffn_routers, L0_ffn_leaf_values,
                          L0_ffn_Wout_w, L0_ffn_Wout_b)
    nc = build_nc()
    in_maps = []
    for b in range(8):
        m = dict(shared)
        m["xT"] = np.ascontiguousarray(x[b].T)
        in_maps.append(m)
    res = run_bass_kernel_spmd(nc, in_maps, core_ids=list(range(8)))
    outs = [np.asarray(r["out"]).T for r in res.results]
    return np.ascontiguousarray(np.stack(outs, axis=0), dtype=np.float32)
